# version 01
# baseline (speedup 1.0000x reference)
import numpy as np
from contextlib import ExitStack
import concourse.bass as bass
import concourse.mybir as mybir
from concourse.bass_utils import run_bass_kernel_spmd

F32 = mybir.dt.float32
BF16 = mybir.dt.bfloat16
AF = mybir.ActivationFunctionType
ALU = mybir.AluOpType

B, T, D, NST, RK, INNER = 2, 8192, 1024, 16, 64, 4096
DSL = 256
Q = 512
NCH = T // Q
TQ = T // 4
FBL = 256
NFB = TQ // FBL
NTT = FBL // 128
MAGIC = float(np.float32(1.5 * 2 ** 23))
EPS = 1e-5
NW = float(INNER * D)
GRPS = [[0, 1, 2, 3], [4, 5, 6, 7]]


class Ledger:

    def __init__(self, sems, engines):
        self.sems = sems
        self.engines = engines
        self.cnt = {e: 0 for e in sems}
        self.mark = {e: {} for e in engines}
        self.buf = {}
        self.queue = {e: [] for e in engines}

    def emit(self, e, fn, reads=(), writes=(), dma=False, chan=None):
        sem_name = chan if (dma and chan) else e
        assert sem_name in self.sems, sem_name
        need = {}
        for k in reads:
            b = self.buf.get(k)
            if b and b["w"] is not None:
                se, sv = b["w"]
                need[se] = max(need.get(se, -1), sv)
        for k in writes:
            b = self.buf.get(k)
            if b:
                if b["w"] is not None:
                    se, sv = b["w"]
                    need[se] = max(need.get(se, -1), sv)
                for se, sv in b["r"].items():
                    need[se] = max(need.get(se, -1), sv)
        for se in sorted(need):
            sv = need[se]
            if se == e == "t":
                continue
            if self.mark[e].get(se, -1) < sv:
                self.queue[e].append(("wait", se, sv))
                self.mark[e][se] = sv
        step = 16 if dma else 1
        self.cnt[sem_name] = self.cnt.get(sem_name, 0) + step
        v = self.cnt[sem_name]
        self.queue[e].append(("op", fn, step, sem_name))
        for k in reads:
            b = self.buf.setdefault(k, {"w": None, "r": {}})
            b["r"][sem_name] = max(b["r"].get(sem_name, -1), v)
        for k in writes:
            self.buf[k] = {"w": (sem_name, v), "r": {}}

    def barrier(self):
        tops = dict(self.cnt)
        for e in self.queue:
            for se, sv in tops.items():
                if se != e and sv > 0 and self.mark[e].get(se, -1) < sv:
                    self.queue[e].append(("wait", se, sv))
                    self.mark[e][se] = sv


def build_program():
    nc = bass.Bass("TRN2", target_bir_lowering=False)

    u_full = nc.dram_tensor("u_full", [T, D], F32, kind="ExternalInput").ap()
    u_tok = nc.dram_tensor("u_tok", [TQ, D], F32, kind="ExternalInput").ap()
    wxdtT_d = nc.dram_tensor("wxdtT", [D, RK], F32, kind="ExternalInput").ap()
    bxdt_d = nc.dram_tensor("bxdt", [RK, 1], F32, kind="ExternalInput").ap()
    wdtT_d = nc.dram_tensor("wdtT_sl", [RK, DSL], F32, kind="ExternalInput").ap()
    bdt_d = nc.dram_tensor("bdt_sl", [DSL, 1], F32, kind="ExternalInput").ap()
    wbT_d = nc.dram_tensor("wbT", [D, NST], F32, kind="ExternalInput").ap()
    bb_d = nc.dram_tensor("bb", [NST, 1], F32, kind="ExternalInput").ap()
    wcT_d = nc.dram_tensor("wcT", [D, NST], F32, kind="ExternalInput").ap()
    bc_d = nc.dram_tensor("bc", [NST, 1], F32, kind="ExternalInput").ap()
    alog_d = nc.dram_tensor("alog_sl", [DSL, NST], F32, kind="ExternalInput").ap()
    dvec_d = nc.dram_tensor("dvec", [1, D], F32, kind="ExternalInput").ap()
    eye_d = nc.dram_tensor("eye", [128, 128], F32, kind="ExternalInput").ap()
    wgT_d = nc.dram_tensor("wgT_q", [D, 1024], F32, kind="ExternalInput").ap()
    wuT_d = nc.dram_tensor("wuT_q", [D, 1024], F32, kind="ExternalInput").ap()
    wdT_d = nc.dram_tensor("wdT_q", [1024, D], F32, kind="ExternalInput").ap()
    out_d = nc.dram_tensor("out", [TQ, D], F32, kind="ExternalOutput").ap()
    bc_B = nc.dram_tensor("bc_B", [NST, T], F32).ap()
    bc_C = nc.dram_tensor("bc_C", [NST, T], F32).ap()
    scal_dr = nc.dram_tensor("scal_dr", [1, 8], F32).ap()
    cc_y_in = [nc.dram_tensor(f"ccyi{k}", [TQ, DSL], F32) for k in range(4)]
    cc_y_out = [nc.dram_tensor(f"ccyo{k}", [4, TQ, DSL], F32) for k in range(4)]
    cc_w_in = nc.dram_tensor("ccwi", [3, 1024, 1024], BF16)
    cc_w_out = nc.dram_tensor("ccwo", [4, 3, 1024, 1024], BF16)
    cc_s_in = nc.dram_tensor("ccsi", [1, 4], F32)
    cc_s_out = nc.dram_tensor("ccso", [1, 4], F32)
    cc_y_sel = nc.dram_tensor("ccysel", [4, TQ, DSL], F32)

    _st = ExitStack()
    with _st:
        eye_sb = _st.enter_context(nc.sbuf_tensor("eye_sb", [128, 128], F32))
        eyeb_sb = _st.enter_context(nc.sbuf_tensor("eyeb_sb", [128, 128], BF16))
        wxdtT_sb = _st.enter_context(nc.sbuf_tensor("wxdtT_sb", [128, 8 * RK], BF16))
        wbT_sb = _st.enter_context(nc.sbuf_tensor("wbT_sb", [128, 8 * NST], BF16))
        wcT_sb = _st.enter_context(nc.sbuf_tensor("wcT_sb", [128, 8 * NST], BF16))
        wdtT_sb = _st.enter_context(nc.sbuf_tensor("wdtT_sb", [RK, DSL], BF16))
        bxdt_sb = _st.enter_context(nc.sbuf_tensor("bxdt_sb", [RK, 1], F32))
        bdt_sb = _st.enter_context(nc.sbuf_tensor("bdt_sb", [128, 2], F32))
        bb_sb = _st.enter_context(nc.sbuf_tensor("bb_sb", [NST, 1], F32))
        bc_sb = _st.enter_context(nc.sbuf_tensor("bc_sb", [NST, 1], F32))
        acol_sb = _st.enter_context(nc.sbuf_tensor("acol_sb", [128, 2 * NST], F32))
        hstate = _st.enter_context(nc.sbuf_tensor("hstate", [128, 2 * NST], F32))
        wsum_sb = _st.enter_context(nc.sbuf_tensor("wsum_sb", [128, 4], F32))
        wrow_sb = _st.enter_context(nc.sbuf_tensor("wrow_sb", [1, 128], F32))
        wsc_sb = _st.enter_context(nc.sbuf_tensor("wsc_sb", [1, 8], F32))
        wscb_sb = _st.enter_context(nc.sbuf_tensor("wscb_sb", [128, 8], F32))
        u_ch = _st.enter_context(nc.sbuf_tensor("u_ch", [128, 2 * D], F32))
        sq_ch = _st.enter_context(nc.sbuf_tensor("sq_ch", [128, D], F32))
        st_ch = _st.enter_context(nc.sbuf_tensor("st_ch", [128, 8], F32))
        xn_ch = _st.enter_context(nc.sbuf_tensor("xn_ch", [128, 2 * D], F32))
        xnT_sb = _st.enter_context(nc.sbuf_tensor("xnT_sb", [128, 8 * Q], BF16))
        xnsl_sb = _st.enter_context(nc.sbuf_tensor("xnsl_sb", [128, 2 * Q], F32))
        xdt_sb = _st.enter_context(nc.sbuf_tensor("xdt_sb", [RK, Q], BF16))
        dt_sb = _st.enter_context(nc.sbuf_tensor("dt_sb", [128, 2 * Q], F32))
        edt_sb = _st.enter_context(nc.sbuf_tensor("edt_sb", [128, Q], F32))
        bcq_sb = _st.enter_context(nc.sbuf_tensor("bcq_sb", [NST, Q], F32))
        ccq_sb = _st.enter_context(nc.sbuf_tensor("ccq_sb", [NST, Q], F32))
        c_sb = _st.enter_context(nc.sbuf_tensor("c_sb", [128, 2 * Q], F32))
        bbc_sb = _st.enter_context(nc.sbuf_tensor("bbc_sb", [128, 2 * Q], F32))
        cbc_sb = _st.enter_context(nc.sbuf_tensor("cbc_sb", [128, 2 * Q], F32))
        abar_sb = _st.enter_context(nc.sbuf_tensor("abar_sb", [128, 2 * Q], F32))
        in1_sb = _st.enter_context(nc.sbuf_tensor("in1_sb", [128, 2 * Q], F32))
        h_sb = _st.enter_context(nc.sbuf_tensor("h_sb", [128, 2 * Q], F32))
        hc_sb = _st.enter_context(nc.sbuf_tensor("hc_sb", [128, Q], F32))
        ych_sb = _st.enter_context(nc.sbuf_tensor("ych_sb", [128, 2 * Q], F32))
        ytok_sb = _st.enter_context(nc.sbuf_tensor("ytok_sb", [128, 4 * DSL], F32))
        x2_sb = _st.enter_context(nc.sbuf_tensor("x2_sb", [128, NTT * D], F32))
        xn2_sb = _st.enter_context(nc.sbuf_tensor("xn2_sb", [128, 4096], F32))
        fdu_sb = _st.enter_context(nc.sbuf_tensor("fdu_sb", [128, D], F32))
        fst_sb = _st.enter_context(nc.sbuf_tensor("fst_sb", [128, 16], F32))
        xq_sb = _st.enter_context(nc.sbuf_tensor("xq_sb", [128, NTT * D], BF16))
        xqT_sb = _st.enter_context(nc.sbuf_tensor("xqT_sb", [128, 8 * FBL], BF16))
        wA_sb = _st.enter_context(nc.sbuf_tensor("wA_sb", [128, 8 * 512], BF16))
        wB_sb = _st.enter_context(nc.sbuf_tensor("wB_sb", [128, 8 * 512], BF16))
        gate_sb = _st.enter_context(nc.sbuf_tensor("gate_sb", [128, 512], F32))
        arena_a = _st.enter_context(nc.sbuf_tensor("arena_a", [128, 8192], F32))
        arena_b = _st.enter_context(nc.sbuf_tensor("arena_b", [128, 8192], F32))
        ot_sb = _st.enter_context(nc.sbuf_tensor("ot_sb", [128, NTT * D], F32))
        g_sb = arena_a.ap()
        wq_ld = arena_a.ap()
        gq_sb = arena_b.ap().bitcast(BF16)[:, 0:NTT * INNER]
        gqT_sb = arena_b.ap().bitcast(BF16)[:, NTT * INNER:2 * NTT * INNER]
        wq_r = arena_b.ap()
        wq_b = xn2_sb.ap().bitcast(BF16)
        yg_sb = u_ch.ap()
        uf_sb = xn_ch.ap()
        fsq_sb = sq_ch.ap()
        dbc_sb = bbc_sb.ap()[:, 0:D]
        ps0 = _st.enter_context(nc.psum_tensor("ps0", [128, 512], F32))
        ps1 = _st.enter_context(nc.psum_tensor("ps1", [128, 512], F32))
        ps2 = _st.enter_context(nc.psum_tensor("ps2", [128, 512], F32))
        ps3 = _st.enter_context(nc.psum_tensor("ps3", [128, 512], F32))
        ps4 = _st.enter_context(nc.psum_tensor("ps4", [128, 512], F32))
        ps5 = _st.enter_context(nc.psum_tensor("ps5", [128, 512], F32))
        ps6 = _st.enter_context(nc.psum_tensor("ps6", [128, 512], F32))
        ps7 = _st.enter_context(nc.psum_tensor("ps7", [128, 512], F32))
        sem_v = _st.enter_context(nc.semaphore("sem_v"))
        sem_s = _st.enter_context(nc.semaphore("sem_s"))
        sem_t = _st.enter_context(nc.semaphore("sem_t"))
        sem_g = _st.enter_context(nc.semaphore("sem_g"))
        sem_q = _st.enter_context(nc.semaphore("sem_q"))
        block = _st.enter_context(nc.Block())
        sems = {"v": sem_v, "s": sem_s, "t": sem_t, "g": sem_g, "q": sem_q}
        CHANS = ["qm", "qwl", "qwr", "qss", "qsc", "qsw", "qu0", "qu1",
                 "qsB", "qsC", "qb0", "qb1", "qc0", "qc1",
                 "qsy0", "qsy1", "qsy2", "qsy3", "qdbg", "qd", "qsel",
                 "qy0", "qy1", "qf0", "qf1", "qw", "qw2", "qso"]
        for cn in CHANS:
            sems[cn] = _st.enter_context(nc.semaphore("sem_" + cn))
        L = Ledger(sems, ["v", "s", "t", "g", "q"])
        E = L.emit
        PID = [None]

        E("q", lambda q: q.dma_start(out=eye_sb[:, :], in_=eye_d[:, :]),
          writes=["eye"], dma=True, chan="qm")
        E("v", lambda v: v.tensor_copy(eyeb_sb[:, :], eye_sb[:, :]),
          reads=["eye"], writes=["eyeb"])
        for nm, dst, src, nk, w in [
            ("wxdtT", wxdtT_sb, wxdtT_d, 8, RK),
            ("wbT", wbT_sb, wbT_d, 8, NST),
            ("wcT", wcT_sb, wcT_d, 8, NST),
        ]:
            E("q", lambda q, src=src, nk=nk, w=w: q.dma_start(
                out=wq_ld[:, 0:nk * w].rearrange("p (k r) -> p k r", k=nk),
                in_=src.rearrange("(k p) r -> p k r", p=128)),
              writes=["wq_ld"], dma=True, chan="qwl")
            E("v", lambda v, dst=dst, nk=nk, w=w: v.tensor_copy(
                dst[:, :], wq_ld[:, 0:nk * w]),
              reads=["wq_ld"], writes=[nm])
        E("q", lambda q: q.dma_start(out=wq_ld[0:RK, 0:DSL], in_=wdtT_d[:, :]),
          writes=["wq_ld"], dma=True, chan="qwl")
        E("v", lambda v: v.tensor_copy(wdtT_sb[:, :], wq_ld[0:RK, 0:DSL]),
          reads=["wq_ld"], writes=["wdtT"])
        E("q", lambda q: q.dma_start(out=bxdt_sb[:, :], in_=bxdt_d[:, :]),
          writes=["bxdt"], dma=True, chan="qm")
        E("q", lambda q: q.dma_start(
            out=bdt_sb[:, :].rearrange("p (d c) -> p d c", d=2),
            in_=bdt_d.rearrange("(d p) c -> p d c", p=128),
            allow_slow_non_contiguous=True),
          writes=["bdt"], dma=True, chan="qm")
        E("q", lambda q: q.dma_start(out=bb_sb[:, :], in_=bb_d[:, :]),
          writes=["bbias"], dma=True, chan="qm")
        E("q", lambda q: q.dma_start(out=bc_sb[:, :], in_=bc_d[:, :]),
          writes=["cbias"], dma=True, chan="qm")
        E("q", lambda q: q.dma_start(
            out=acol_sb[:, :].rearrange("p (d n) -> p d n", d=2),
            in_=alog_d.rearrange("(d p) n -> p d n", p=128)),
          writes=["acol"], dma=True, chan="qm")
        E("s", lambda s: s.activation(acol_sb[:, :], acol_sb[:, :], AF.Exp),
          reads=["acol"], writes=["acol"])
        E("v", lambda v: v.tensor_scalar_mul(acol_sb[:, :], acol_sb[:, :], -1.0),
          reads=["acol"], writes=["acol"])
        E("v", lambda v: v.memset(hstate[:, :], 0.0),
          writes=[f"hst{i}" for i in range(2 * NST)])

        wsrc = [wgT_d, wuT_d, wdT_d]
        for m in range(3):
            E("q", lambda q, m=m: q.dma_start(
                out=wq_ld[:, :].rearrange("p (k c) -> p k c", k=8),
                in_=wsrc[m].rearrange("(k p) c -> p k c", p=128)),
              writes=["wq_ld"], dma=True, chan="qwl")
            E("s", lambda s, m=m: s.activation(
                wq_r[:, :], wq_ld[:, :], AF.Abs,
                accum_out=wsum_sb[:, m:m + 1]),
              reads=["wq_ld"], writes=["wq_r", f"wsum{m}"])
        for m in range(3):
            E("q", lambda q, m=m: q.dma_start(
                out=wrow_sb[:, :], in_=wsum_sb[:, m:m + 1],
                allow_slow_non_contiguous=True),
              reads=[f"wsum{m}"], writes=["wrow"], dma=True, chan="qwr")
            E("v", lambda v, m=m: v.reduce_sum(
                wsc_sb[0:1, m:m + 1], wrow_sb[:, :], axis=mybir.AxisListType.X),
              reads=["wrow"], writes=["wsc"])
        E("q", lambda q: q.dma_start(out=cc_s_in[0:1, :], in_=wsc_sb[0:1, 0:4]),
          reads=["wsc"], writes=["ccsi"], dma=True, chan="qss")
        E("g", lambda g: g.collective_compute(
            "AllReduce", ALU.add, replica_groups=GRPS,
            ins=[cc_s_in.ap().opt()], outs=[cc_s_out.ap().opt()]),
          reads=["ccsi"], writes=["ccso"])
        E("q", lambda q: q.dma_start(out=wsc_sb[0:1, 0:4], in_=cc_s_out[0:1, :]),
          reads=["ccso"], writes=["wsc"], dma=True, chan="qss")
        E("v", lambda v: v.tensor_scalar(
            wsc_sb[0:1, 4:8], wsc_sb[0:1, 0:4], 1.0 / NW, 1e-5,
            op0=ALU.mult, op1=ALU.max),
          reads=["wsc"], writes=["wsc"])
        E("v", lambda v: v.reciprocal(wsc_sb[0:1, 0:4], wsc_sb[0:1, 4:8]),
          reads=["wsc"], writes=["wsc"])
        E("q", lambda q: q.dma_start(out=scal_dr[0:1, :], in_=wsc_sb[0:1, :]),
          reads=["wsc"], writes=["scal"], dma=True, chan="qsc")
        E("q", lambda q: q.dma_start(out=wscb_sb[:, :],
                                     in_=scal_dr.broadcast_to([128, 8])),
          reads=["scal"], writes=["wscb"], dma=True, chan="qsc")
        for m in range(3):
            E("q", lambda q, m=m: q.dma_start(
                out=wq_ld[:, :].rearrange("p (k c) -> p k c", k=8),
                in_=wsrc[m].rearrange("(k p) c -> p k c", p=128)),
              writes=["wq_ld"], dma=True, chan="qwl")
            E("v", lambda v, m=m: v.tensor_scalar(
                wq_r[:, :], wq_ld[:, :], wscb_sb[:, m:m + 1], MAGIC,
                op0=ALU.mult, op1=ALU.add),
              reads=["wq_ld", "wscb"], writes=["wq_r"])
            E("v", lambda v: v.tensor_scalar(
                wq_r[:, :], wq_r[:, :], MAGIC, None, op0=ALU.subtract),
              reads=["wq_r"], writes=["wq_r"])
            E("v", lambda v: v.tensor_scalar(
                wq_b[:, :], wq_r[:, :], 1.0, -1.0, op0=ALU.min, op1=ALU.max),
              reads=["wq_r"], writes=["wq_b"])
            E("q", lambda q, m=m: q.dma_start(
                out=cc_w_in[m, :, :].rearrange("(k p) c -> p k c", p=128),
                in_=wq_b[:, :].rearrange("p (k c) -> p k c", k=8)),
              reads=["wq_b"], writes=["ccwi"], dma=True, chan="qsw")
        E("g", lambda g: g.collective_compute(
            "AllGather", ALU.bypass, replica_groups=GRPS,
            ins=[cc_w_in.ap().opt()], outs=[cc_w_out.ap().opt()]),
          reads=["ccwi"], writes=["ccwo"])

        psT = [ps3, ps4]

        def ln_apply(src_key, src_tile, st, dst_key, dst_tile):
            E("v", lambda v: v.reduce_sum(
                st[:, 0:1], src_tile, axis=mybir.AxisListType.X),
              reads=[src_key], writes=["stm"])
            E("s", lambda s: s.activation(
                sq_ch[:, :], src_tile, AF.Square, accum_out=st[:, 2:3]),
              reads=[src_key], writes=["stsq", "sts2"])
            E("v", lambda v: v.tensor_scalar(
                st[:, 0:1], st[:, 0:1], 1.0 / D, None, op0=ALU.mult),
              reads=["stm"], writes=["stm"])
            E("v", lambda v: v.tensor_scalar(
                st[:, 2:3], st[:, 2:3], 1.0 / D, EPS,
                op0=ALU.mult, op1=ALU.add),
              reads=["sts2"], writes=["sts2"])
            E("v", lambda v: v.tensor_tensor(
                st[:, 3:4], st[:, 0:1], st[:, 0:1], op=ALU.mult),
              reads=["stm"], writes=["stm2"])
            E("v", lambda v: v.tensor_tensor(
                st[:, 1:2], st[:, 2:3], st[:, 3:4], op=ALU.subtract),
              reads=["sts2", "stm2"], writes=["str"])
            E("s", lambda s: s.activation(st[:, 1:2], st[:, 1:2], AF.Ln),
              reads=["str"], writes=["str"])
            E("s", lambda s: s.activation(st[:, 1:2], st[:, 1:2], AF.Exp,
                                          scale=-0.5),
              reads=["str"], writes=["str"])
            E("v", lambda v: v.tensor_scalar(
                dst_tile, src_tile, st[:, 0:1], st[:, 1:2],
                op0=ALU.subtract, op1=ALU.mult),
              reads=[src_key, "stm", "str"], writes=[dst_key])

        for ch in range(NCH):
            tok0 = ch * Q
            kq = ch // (NCH // 4)
            for tt in range(4):
                us = tt % 2
                ut = u_ch[:, us * D:(us + 1) * D]
                xt_ = xn_ch[:, us * D:(us + 1) * D]
                E("q", lambda q, tok0=tok0, tt=tt, ut=ut: q.dma_start(
                    out=ut, in_=u_full[tok0 + tt * 128: tok0 + (tt + 1) * 128, :]),
                  writes=[f"u{us}"], dma=True, chan=f"qu{us}")
                ln_apply(f"u{us}", ut, st_ch, f"xn{us}", xt_)
                for k in range(8):
                    pst = psT[k % 2]
                    E("t", lambda t, xt_=xt_, k=k, pst=pst: t.transpose(
                        pst[:, 0:128], xt_[:, k * 128:(k + 1) * 128],
                        eye_sb[:, :]),
                      reads=[f"xn{us}", "eye"], writes=[f"psT{k % 2}"])
                    E("s", lambda s, k=k, tt=tt, pst=pst: s.activation(
                        xnT_sb[:, k * Q + tt * 128: k * Q + (tt + 1) * 128],
                        pst[:, 0:128], AF.Copy),
                      reads=[f"psT{k % 2}"], writes=[f"xnT{k}"])
                    if k < 2:
                        E("s", lambda s, k=k, tt=tt, pst=pst: s.activation(
                            xnsl_sb[:, k * Q + tt * 128: k * Q + (tt + 1) * 128],
                            pst[:, 0:128], AF.Copy),
                          reads=[f"psT{k % 2}"], writes=[f"xnsl{k}"])
            for k in range(8):
                E("t", lambda t, k=k: t.matmul(
                    ps0[0:RK, 0:Q], wxdtT_sb[:, k * RK:(k + 1) * RK],
                    xnT_sb[:, k * Q:(k + 1) * Q],
                    start=(k == 0), stop=(k == 7)),
                  reads=[f"xnT{k}", "wxdtT"], writes=["ps0"])
            E("s", lambda s: s.activation(
                xdt_sb[:, :], ps0[0:RK, 0:Q], AF.Identity, bias=bxdt_sb[:, :]),
              reads=["ps0", "bxdt"], writes=["xdt"])
            for k in range(8):
                E("t", lambda t, k=k: t.matmul(
                    ps0[0:NST, 0:Q], wbT_sb[:, k * NST:(k + 1) * NST],
                    xnT_sb[:, k * Q:(k + 1) * Q],
                    start=(k == 0), stop=(k == 7)),
                  reads=[f"xnT{k}", "wbT"], writes=["ps0"])
            E("s", lambda s: s.activation(
                bcq_sb[:, :], ps0[0:NST, 0:Q], AF.Identity, bias=bb_sb[:, :]),
              reads=["ps0", "bbias"], writes=["bcq"])
            E("q", lambda q, tok0=tok0: q.dma_start(
                out=bc_B[:, tok0:tok0 + Q], in_=bcq_sb[:, :]),
              reads=["bcq"], writes=["bcBdr"], dma=True, chan="qsB")
            for k in range(8):
                E("t", lambda t, k=k: t.matmul(
                    ps0[0:NST, 0:Q], wcT_sb[:, k * NST:(k + 1) * NST],
                    xnT_sb[:, k * Q:(k + 1) * Q],
                    start=(k == 0), stop=(k == 7)),
                  reads=[f"xnT{k}", "wcT"], writes=["ps0"])
            E("s", lambda s: s.activation(
                ccq_sb[:, :], ps0[0:NST, 0:Q], AF.Identity, bias=bc_sb[:, :]),
              reads=["ps0", "cbias"], writes=["ccq"])
            E("q", lambda q, tok0=tok0: q.dma_start(
                out=bc_C[:, tok0:tok0 + Q], in_=ccq_sb[:, :]),
              reads=["ccq"], writes=["bcCdr"], dma=True, chan="qsC")
            for dti in range(2):
                psd = [ps1, ps2][dti]
                E("t", lambda t, dti=dti, psd=psd: t.matmul(
                    psd[:, 0:Q], wdtT_sb[:, dti * 128:(dti + 1) * 128],
                    xdt_sb[:, :], start=True, stop=True),
                  reads=["xdt", "wdtT"], writes=[f"psd{dti}"])
                E("s", lambda s, dti=dti, psd=psd: s.activation(
                    edt_sb[:, :], psd[:, 0:Q], AF.Exp,
                    bias=bdt_sb[:, dti:dti + 1]),
                  reads=[f"psd{dti}", "bdt"], writes=["edt"])
                E("s", lambda s, dti=dti: s.activation(
                    dt_sb[:, dti * Q:(dti + 1) * Q], edt_sb[:, :], AF.Ln,
                    bias=1.0),
                  reads=["edt"], writes=[f"dt{dti}"])
                E("g", lambda g, dti=dti: g.tensor_tensor(
                    c_sb[:, dti * Q:(dti + 1) * Q],
                    dt_sb[:, dti * Q:(dti + 1) * Q],
                    xnsl_sb[:, dti * Q:(dti + 1) * Q], op=ALU.mult),
                  reads=[f"dt{dti}", f"xnsl{dti}"], writes=[f"c{dti}"])
            for n in range(NST):
                bsl = n % 2
                E("q", lambda q, n=n, tok0=tok0, bsl=bsl: q.dma_start(
                    out=bbc_sb[:, bsl * Q:(bsl + 1) * Q],
                    in_=bc_B[n:n + 1, tok0:tok0 + Q].broadcast_to([128, Q])),
                  reads=["bcBdr"], writes=[f"bbc{bsl}"], dma=True, chan=f"qb{bsl}")
                E("q", lambda q, n=n, tok0=tok0, bsl=bsl: q.dma_start(
                    out=cbc_sb[:, bsl * Q:(bsl + 1) * Q],
                    in_=bc_C[n:n + 1, tok0:tok0 + Q].broadcast_to([128, Q])),
                  reads=["bcCdr"], writes=[f"cbc{bsl}"], dma=True, chan=f"qc{bsl}")
                for dti in range(2):
                    asl = (2 * n + dti) % 2
                    col = dti * NST + n
                    E("s", lambda s, n=n, dti=dti, asl=asl: s.activation(
                        abar_sb[:, asl * Q:(asl + 1) * Q],
                        dt_sb[:, dti * Q:(dti + 1) * Q], AF.Exp,
                        scale=acol_sb[:, dti * NST + n:dti * NST + n + 1]),
                      reads=[f"dt{dti}", "acol"], writes=[f"abar{asl}"])
                    E("g", lambda g, dti=dti, asl=asl, bsl=bsl: g.tensor_tensor(
                        in1_sb[:, asl * Q:(asl + 1) * Q],
                        c_sb[:, dti * Q:(dti + 1) * Q],
                        bbc_sb[:, bsl * Q:(bsl + 1) * Q], op=ALU.mult),
                      reads=[f"c{dti}", f"bbc{bsl}"], writes=[f"in1{asl}"])
                    E("v", lambda v, asl=asl, col=col: v.tensor_tensor_scan(
                        h_sb[:, asl * Q:(asl + 1) * Q],
                        abar_sb[:, asl * Q:(asl + 1) * Q],
                        in1_sb[:, asl * Q:(asl + 1) * Q],
                        hstate[:, col:col + 1], op0=ALU.mult, op1=ALU.add),
                      reads=[f"abar{asl}", f"in1{asl}", f"hst{col}"],
                      writes=[f"h{asl}"])
                    E("v", lambda v, asl=asl, col=col: v.tensor_copy(
                        hstate[:, col:col + 1],
                        h_sb[:, asl * Q + Q - 1: asl * Q + Q]),
                      reads=[f"h{asl}"], writes=[f"hst{col}"])
                    if n == 0:
                        E("v", lambda v, asl=asl, bsl=bsl, dti=dti:
                          v.tensor_tensor(
                              ych_sb[:, dti * Q:(dti + 1) * Q],
                              h_sb[:, asl * Q:(asl + 1) * Q],
                              cbc_sb[:, bsl * Q:(bsl + 1) * Q], op=ALU.mult),
                          reads=[f"h{asl}", f"cbc{bsl}"], writes=[f"ych{dti}"])
                    else:
                        E("v", lambda v, asl=asl, bsl=bsl: v.tensor_tensor(
                            hc_sb[:, :], h_sb[:, asl * Q:(asl + 1) * Q],
                            cbc_sb[:, bsl * Q:(bsl + 1) * Q], op=ALU.mult),
                          reads=[f"h{asl}", f"cbc{bsl}"], writes=["hc"])
                        E("v", lambda v, dti=dti: v.tensor_tensor(
                            ych_sb[:, dti * Q:(dti + 1) * Q],
                            ych_sb[:, dti * Q:(dti + 1) * Q], hc_sb[:, :],
                            op=ALU.add),
                          reads=["hc", f"ych{dti}"], writes=[f"ych{dti}"])
            for tt in range(4):
                for dti in range(2):
                    pst = psT[dti]
                    E("t", lambda t, dti=dti, tt=tt, pst=pst: t.transpose(
                        pst[:, 0:128],
                        ych_sb[:, dti * Q + tt * 128: dti * Q + (tt + 1) * 128],
                        eye_sb[:, :]),
                      reads=[f"ych{dti}", "eye"], writes=[f"psT{dti}"])
                    E("s", lambda s, dti=dti, tt=tt, pst=pst: s.activation(
                        ytok_sb[:, tt * DSL + dti * 128:
                                tt * DSL + (dti + 1) * 128],
                        pst[:, 0:128], AF.Copy),
                      reads=[f"psT{dti}"], writes=[f"ytok{tt}"])
                E("q", lambda q, tt=tt, tok0=tok0, kq=kq: q.dma_start(
                    out=cc_y_in[kq][(tok0 % TQ) + tt * 128:
                                    (tok0 % TQ) + (tt + 1) * 128, :],
                    in_=ytok_sb[:, tt * DSL:(tt + 1) * DSL]),
                  reads=[f"ytok{tt}"], writes=[f"ccy{kq}"], dma=True, chan=f"qsy{kq}")
            if ch % (NCH // 4) == (NCH // 4) - 1:
                E("g", lambda g, kq=kq: g.collective_compute(
                    "AllGather", ALU.bypass, replica_groups=GRPS,
                    ins=[cc_y_in[kq].ap().opt()],
                    outs=[cc_y_out[kq].ap().opt()]),
                  reads=[f"ccy{kq}"], writes=[f"ccyog{kq}"])

        L.barrier()

        E("q", lambda q: q.dma_start(out=dbc_sb[:, :],
                                     in_=dvec_d.broadcast_to([128, D])),
          writes=["dbc"], dma=True, chan="qd")
        for k in range(4):
            E("q", lambda q, k=k: q.dma_start(
                out=cc_y_sel.ap(), in_=cc_y_out[k].ap(), cond=PID[0][k]),
              reads=[f"ccyog{k}"], writes=["ccysel"], dma=True, chan="qsel")
        def ln2_apply(src_key, src_tile, st, dst_key, dst_tile, scr):
            E("v", lambda v: v.reduce_sum(
                st[:, 0:1], src_tile, axis=mybir.AxisListType.X),
              reads=[src_key], writes=["fm"])
            E("s", lambda s: s.activation(
                scr[:, :], src_tile, AF.Square, accum_out=st[:, 2:3]),
              reads=[src_key], writes=["fdu", "fs2"])
            E("v", lambda v: v.tensor_scalar(
                st[:, 0:1], st[:, 0:1], 1.0 / D, None, op0=ALU.mult),
              reads=["fm"], writes=["fm"])
            E("v", lambda v: v.tensor_scalar(
                st[:, 2:3], st[:, 2:3], 1.0 / D, EPS, op0=ALU.mult, op1=ALU.add),
              reads=["fs2"], writes=["fs2"])
            E("v", lambda v: v.tensor_tensor(
                st[:, 3:4], st[:, 0:1], st[:, 0:1], op=ALU.mult),
              reads=["fm"], writes=["fm2"])
            E("v", lambda v: v.tensor_tensor(
                st[:, 1:2], st[:, 2:3], st[:, 3:4], op=ALU.subtract),
              reads=["fs2", "fm2"], writes=["fr"])
            E("s", lambda s: s.activation(st[:, 1:2], st[:, 1:2], AF.Ln),
              reads=["fr"], writes=["fr"])
            E("s", lambda s: s.activation(st[:, 1:2], st[:, 1:2], AF.Exp,
                                          scale=-0.5),
              reads=["fr"], writes=["fr"])
            E("v", lambda v: v.tensor_scalar(
                dst_tile, src_tile, st[:, 0:1], st[:, 1:2],
                op0=ALU.subtract, op1=ALU.mult),
              reads=[src_key, "fm", "fr"], writes=[dst_key])

        psG = [ps1, ps5]
        psU = [ps2, ps6]
        psD_ = [ps7, ps0]

        for bl in range(NFB):
            fb0 = bl * FBL
            for tt in range(NTT):
                r0 = fb0 + tt * 128
                E("q", lambda q, r0=r0, tt=tt: q.dma_start(
                    out=yg_sb[:, tt * D:(tt + 1) * D]
                    .rearrange("p (j c) -> p j c", j=4),
                    in_=cc_y_sel.ap()[:, r0:r0 + 128, :]
                    .transpose([1, 0, 2])),
                  reads=["ccysel"], writes=[f"yg{tt}"], dma=True, chan=f"qy{tt}")
                E("q", lambda q, r0=r0, tt=tt: q.dma_start(
                    out=uf_sb[:, tt * D:(tt + 1) * D],
                    in_=u_tok[r0:r0 + 128, :]),
                  writes=[f"uf{tt}"], dma=True, chan=f"qf{tt}")
                uft = uf_sb[:, tt * D:(tt + 1) * D]
                ln2_apply(f"uf{tt}", uft, fst_sb, "fsq", fsq_sb[:, :],
                          fdu_sb)
                E("v", lambda v: v.tensor_tensor(
                    fdu_sb[:, :], fsq_sb[:, :], dbc_sb[:, :], op=ALU.mult),
                  reads=["fsq", "dbc"], writes=["fdu"])
                E("v", lambda v, tt=tt, uft=uft: v.tensor_tensor(
                    x2_sb[:, tt * D:(tt + 1) * D], uft,
                    yg_sb[:, tt * D:(tt + 1) * D], op=ALU.add),
                  reads=[f"uf{tt}", f"yg{tt}"], writes=[f"x2{tt}"])
                E("v", lambda v, tt=tt: v.tensor_tensor(
                    x2_sb[:, tt * D:(tt + 1) * D],
                    x2_sb[:, tt * D:(tt + 1) * D], fdu_sb[:, :], op=ALU.add),
                  reads=["fdu", f"x2{tt}"], writes=[f"x2{tt}"])
                ln2_apply(f"x2{tt}", x2_sb[:, tt * D:(tt + 1) * D], fst_sb,
                          "fsq", fsq_sb[:, :], fdu_sb)
                E("v", lambda v: v.reduce_max(
                    fst_sb[:, 4:5], fsq_sb[:, :], axis=mybir.AxisListType.X,
                    apply_absolute_value=True),
                  reads=["fsq"], writes=["fmax"])
                E("v", lambda v: v.tensor_scalar_max(
                    fst_sb[:, 4:5], fst_sb[:, 4:5], 1e-5),
                  reads=["fmax"], writes=["fmax"])
                E("v", lambda v: v.reciprocal(fst_sb[:, 5:6], fst_sb[:, 4:5]),
                  reads=["fmax"], writes=["fsx"])
                E("v", lambda v: v.tensor_scalar(
                    fst_sb[:, 5:6], fst_sb[:, 5:6], 127.0, None, op0=ALU.mult),
                  reads=["fsx"], writes=["fsx"])
                E("v", lambda v, tt=tt: v.tensor_scalar(
                    fst_sb[:, 6 + tt:7 + tt], fst_sb[:, 4:5], 1.0 / 127.0, None,
                    op0=ALU.mult),
                  reads=["fmax"], writes=[f"fisx{tt}"])
                E("v", lambda v: v.tensor_scalar(
                    fdu_sb[:, :], fsq_sb[:, :], fst_sb[:, 5:6], MAGIC,
                    op0=ALU.mult, op1=ALU.add),
                  reads=["fsq", "fsx"], writes=["fdu"])
                E("v", lambda v, tt=tt: v.tensor_scalar(
                    xq_sb[:, tt * D:(tt + 1) * D], fdu_sb[:, :], MAGIC, None,
                    op0=ALU.subtract),
                  reads=["fdu"], writes=[f"xq{tt}"])
                for k in range(8):
                    pst = psT[k % 2]
                    E("t", lambda t, k=k, tt=tt, pst=pst: t.transpose(
                        pst.bitcast(BF16)[:, 0:128],
                        xq_sb[:, tt * D + k * 128: tt * D + (k + 1) * 128],
                        eyeb_sb[:, :]),
                      reads=[f"xq{tt}", "eyeb"], writes=[f"psT{k % 2}"])
                    E("s", lambda s, k=k, tt=tt, pst=pst: s.activation(
                        xqT_sb[:, k * FBL + tt * 128: k * FBL + (tt + 1) * 128],
                        pst.bitcast(BF16)[:, 0:128], AF.Copy),
                      reads=[f"psT{k % 2}"], writes=[f"xqT{k}"])
                E("v", lambda v, tt=tt: v.tensor_tensor(
                    fst_sb[:, 8 + tt:9 + tt], fst_sb[:, 6 + tt:7 + tt],
                    wscb_sb[:, 4:5], op=ALU.mult),
                  reads=[f"fisx{tt}", "wscb"], writes=[f"sgv{tt}"])
                E("v", lambda v, tt=tt: v.tensor_tensor(
                    fst_sb[:, 10 + tt:11 + tt], fst_sb[:, 6 + tt:7 + tt],
                    wscb_sb[:, 5:6], op=ALU.mult),
                  reads=[f"fisx{tt}", "wscb"], writes=[f"suv{tt}"])
            for ic in range(8):
                E("q", lambda q, ic=ic: q.dma_start(
                    out=wA_sb[:, :].rearrange("p (k c) -> p k c", k=8),
                    in_=cc_w_out[ic // 2, 0, :, (ic % 2) * 512:(ic % 2 + 1) * 512]
                    .rearrange("(k p) c -> p k c", p=128)),
                  reads=["ccwo"], writes=["wA"], dma=True, chan="qw")
                E("q", lambda q, ic=ic: q.dma_start(
                    out=wB_sb[:, :].rearrange("p (k c) -> p k c", k=8),
                    in_=cc_w_out[ic // 2, 1, :, (ic % 2) * 512:(ic % 2 + 1) * 512]
                    .rearrange("(k p) c -> p k c", p=128)),
                  reads=["ccwo"], writes=["wB"], dma=True, chan="qw2")
                for tt in range(NTT):
                    for k in range(8):
                        E("t", lambda t, k=k, tt=tt: t.matmul(
                            psG[tt][:, :],
                            xqT_sb[:, k * FBL + tt * 128:
                                   k * FBL + (tt + 1) * 128],
                            wA_sb[:, k * 512:(k + 1) * 512],
                            start=(k == 0), stop=(k == 7)),
                          reads=[f"xqT{k}", "wA"], writes=[f"psG{tt}"])
                    for k in range(8):
                        E("t", lambda t, k=k, tt=tt: t.matmul(
                            psU[tt][:, :],
                            xqT_sb[:, k * FBL + tt * 128:
                                   k * FBL + (tt + 1) * 128],
                            wB_sb[:, k * 512:(k + 1) * 512],
                            start=(k == 0), stop=(k == 7)),
                          reads=[f"xqT{k}", "wB"], writes=[f"psU{tt}"])
                    E("s", lambda s, tt=tt: s.activation(
                        gate_sb[:, :], psG[tt][:, :], AF.Silu,
                        scale=fst_sb[:, 8 + tt:9 + tt]),
                      reads=[f"psG{tt}", f"sgv{tt}"], writes=["gate"])
                    E("v", lambda v, tt=tt, ic=ic: v.scalar_tensor_tensor(
                        g_sb[:, tt * INNER + ic * 512:
                             tt * INNER + (ic + 1) * 512],
                        psU[tt][:, :], fst_sb[:, 10 + tt:11 + tt],
                        gate_sb[:, :], op0=ALU.mult, op1=ALU.mult),
                      reads=[f"psU{tt}", f"suv{tt}", "gate"],
                      writes=[f"g{tt}_{ic}"])
            for tt in range(NTT):
                E("v", lambda v, tt=tt: v.reduce_max(
                    fst_sb[:, 12 + tt:13 + tt],
                    g_sb[:, tt * INNER:(tt + 1) * INNER],
                    axis=mybir.AxisListType.X, apply_absolute_value=True),
                  reads=[f"g{tt}_{ic}" for ic in range(8)], writes=[f"gmax{tt}"])
                E("v", lambda v, tt=tt: v.tensor_scalar_max(
                    fst_sb[:, 12 + tt:13 + tt], fst_sb[:, 12 + tt:13 + tt],
                    1e-5),
                  reads=[f"gmax{tt}"], writes=[f"gmax{tt}"])
                E("v", lambda v, tt=tt: v.reciprocal(
                    fst_sb[:, 14 + tt:15 + tt], fst_sb[:, 12 + tt:13 + tt]),
                  reads=[f"gmax{tt}"], writes=[f"gsx{tt}"])
                E("v", lambda v, tt=tt: v.tensor_scalar(
                    fst_sb[:, 14 + tt:15 + tt], fst_sb[:, 14 + tt:15 + tt],
                    127.0, None, op0=ALU.mult),
                  reads=[f"gsx{tt}"], writes=[f"gsx{tt}"])
                E("v", lambda v, tt=tt: v.tensor_scalar(
                    fst_sb[:, 12 + tt:13 + tt], fst_sb[:, 12 + tt:13 + tt],
                    1.0 / 127.0, None, op0=ALU.mult),
                  reads=[f"gmax{tt}"], writes=[f"gisx{tt}"])
                E("v", lambda v, tt=tt: v.tensor_tensor(
                    fst_sb[:, 12 + tt:13 + tt], fst_sb[:, 12 + tt:13 + tt],
                    wscb_sb[:, 6:7], op=ALU.mult),
                  reads=[f"gisx{tt}", "wscb"], writes=[f"sdv{tt}"])
                E("v", lambda v, tt=tt: v.tensor_scalar(
                    xn2_sb[:, :], g_sb[:, tt * INNER:(tt + 1) * INNER],
                    fst_sb[:, 14 + tt:15 + tt], MAGIC,
                    op0=ALU.mult, op1=ALU.add),
                  reads=[f"g{tt}_{ic}" for ic in range(8)] + [f"gsx{tt}"],
                  writes=["gqr"])
                E("v", lambda v, tt=tt: v.tensor_scalar(
                    gq_sb[:, tt * INNER:(tt + 1) * INNER], xn2_sb[:, :],
                    MAGIC, None, op0=ALU.subtract),
                  reads=["gqr"], writes=[f"gq{tt}"])
                for k in range(32):
                    pst = psT[k % 2]
                    E("t", lambda t, k=k, tt=tt, pst=pst: t.transpose(
                        pst.bitcast(BF16)[:, 0:128],
                        gq_sb[:, tt * INNER + k * 128:
                              tt * INNER + (k + 1) * 128],
                        eyeb_sb[:, :]),
                      reads=[f"gq{tt}", "eyeb"], writes=[f"psT{k % 2}"])
                    E("s", lambda s, k=k, tt=tt, pst=pst: s.activation(
                        gqT_sb[:, k * FBL + tt * 128: k * FBL + (tt + 1) * 128],
                        pst.bitcast(BF16)[:, 0:128], AF.Copy),
                      reads=[f"psT{k % 2}"], writes=[f"gqT{k}_{tt}"])
            for dh in range(2):
                for kg in range(4):
                    E("q", lambda q, dh=dh, kg=kg: q.dma_start(
                        out=wA_sb[:, :].rearrange("p (k c) -> p k c", k=8),
                        in_=cc_w_out[kg, 2, :, dh * 512:(dh + 1) * 512]
                        .rearrange("(k p) c -> p k c", p=128)),
                      reads=["ccwo"], writes=["wA"], dma=True, chan="qw")
                    for tt in range(NTT):
                        for k8 in range(8):
                            k = kg * 8 + k8
                            E("t", lambda t, k=k, k8=k8, tt=tt, dh=dh:
                              t.matmul(
                                  psD_[tt][:, :],
                                  gqT_sb[:, k * FBL + tt * 128:
                                         k * FBL + (tt + 1) * 128],
                                  wA_sb[:, k8 * 512:(k8 + 1) * 512],
                                  start=(k == 0), stop=(k == 31)),
                              reads=[f"gqT{k}_{tt}", "wA"],
                              writes=[f"psD{tt}"])
                for tt in range(NTT):
                    E("v", lambda v, tt=tt, dh=dh: v.scalar_tensor_tensor(
                        ot_sb[:, tt * D + dh * 512: tt * D + (dh + 1) * 512],
                        psD_[tt][:, :], fst_sb[:, 12 + tt:13 + tt],
                        x2_sb[:, tt * D + dh * 512: tt * D + (dh + 1) * 512],
                        op0=ALU.mult, op1=ALU.add),
                      reads=[f"psD{tt}", f"sdv{tt}", f"x2{tt}"],
                      writes=[f"ot{tt}_{dh}"])
            for tt in range(NTT):
                E("q", lambda q, tt=tt, fb0=fb0: q.dma_start(
                    out=out_d[fb0 + tt * 128: fb0 + (tt + 1) * 128, :],
                    in_=ot_sb[:, tt * D:(tt + 1) * D]),
                  reads=[f"ot{tt}_0", f"ot{tt}_1"], writes=["outdr"], dma=True, chan="qso")

        L.barrier()

        def play(engine, name):
            if name == "q":
                pid = engine.partition_id()
                PID[0] = [(pid % 4) == kk for kk in range(4)]
            for item in L.queue[name]:
                if item[0] == "wait":
                    engine.wait_ge(sems[item[1]], item[2])
                else:
                    _, fn, step, sem_name = item
                    fn(engine).then_inc(sems[sem_name], step)

        @block.vector
        def _(v):
            play(v, "v")

        @block.scalar
        def _(s):
            play(s, "s")

        @block.tensor
        def _(t):
            play(t, "t")

        @block.gpsimd
        def _(g):
            play(g, "g")

        @block.sync
        def _(q):
            play(q, "q")

    return nc


_CACHE = {}


def kernel(L0_attn_u, L0_attn_A_log, L0_attn_W_xdt, L0_attn_b_xdt,
           L0_attn_W_dt, L0_attn_b_dt, L0_attn_W_B, L0_attn_b_B,
           L0_attn_W_C, L0_attn_b_C, L0_attn_D, L0_ffn_w_gate,
           L0_ffn_w_up, L0_ffn_w_down):
    u = np.ascontiguousarray(np.asarray(L0_attn_u, dtype=np.float32))
    eye = np.eye(128, dtype=np.float32)
    in_maps = []
    for c in range(8):
        b, q = c // 4, c % 4
        perm = np.r_[np.arange(q * DSL, (q + 1) * DSL),
                     np.arange(0, q * DSL), np.arange((q + 1) * DSL, D)]
        im = {
            "u_full": np.ascontiguousarray(u[b][:, perm]),
            "u_tok": np.ascontiguousarray(u[b][q * TQ:(q + 1) * TQ, :]),
            "wxdtT": np.ascontiguousarray(
                np.asarray(L0_attn_W_xdt, np.float32).T[perm, :]),
            "bxdt": np.asarray(L0_attn_b_xdt, np.float32).reshape(RK, 1),
            "wdtT_sl": np.ascontiguousarray(
                np.asarray(L0_attn_W_dt, np.float32)[q * DSL:(q + 1) * DSL, :].T),
            "bdt_sl": np.ascontiguousarray(
                np.asarray(L0_attn_b_dt, np.float32)[q * DSL:(q + 1) * DSL]
                .reshape(DSL, 1)),
            "wbT": np.ascontiguousarray(
                np.asarray(L0_attn_W_B, np.float32).T[perm, :]),
            "bb": np.asarray(L0_attn_b_B, np.float32).reshape(NST, 1),
            "wcT": np.ascontiguousarray(
                np.asarray(L0_attn_W_C, np.float32).T[perm, :]),
            "bc": np.asarray(L0_attn_b_C, np.float32).reshape(NST, 1),
            "alog_sl": np.ascontiguousarray(
                np.asarray(L0_attn_A_log, np.float32)[q * DSL:(q + 1) * DSL, :]),
            "dvec": np.asarray(L0_attn_D, np.float32).reshape(1, D),
            "eye": eye,
            "wgT_q": np.ascontiguousarray(
                np.asarray(L0_ffn_w_gate, np.float32).T[:, q * 1024:(q + 1) * 1024]),
            "wuT_q": np.ascontiguousarray(
                np.asarray(L0_ffn_w_up, np.float32).T[:, q * 1024:(q + 1) * 1024]),
            "wdT_q": np.ascontiguousarray(
                np.asarray(L0_ffn_w_down, np.float32).T[q * 1024:(q + 1) * 1024, :]),
        }
        in_maps.append(im)
    if "nc" not in _CACHE:
        _CACHE["nc"] = build_program()
    nc = _CACHE["nc"]
    res = run_bass_kernel_spmd(nc, in_maps, core_ids=list(range(8)))
    _CACHE["res"] = res
    out = np.empty((B, T, D), dtype=np.float32)
    for c in range(8):
        b, q = c // 4, c % 4
        out[b, q * TQ:(q + 1) * TQ, :] = res.results[c]["out"]
    return out


# version 02
# speedup vs baseline: 1.0457x; 1.0457x over previous
import numpy as np
from contextlib import ExitStack
import concourse.bass as bass
import concourse.mybir as mybir
from concourse.bass_utils import run_bass_kernel_spmd

F32 = mybir.dt.float32
BF16 = mybir.dt.bfloat16
AF = mybir.ActivationFunctionType
ALU = mybir.AluOpType

B, T, D, NST, RK, INNER = 2, 8192, 1024, 16, 64, 4096
DSL = 256
Q = 512
NCH = T // Q
TQ = T // 4
FBL = 256
NFB = TQ // FBL
NTT = FBL // 128
MAGIC = float(np.float32(1.5 * 2 ** 23))
EPS = 1e-5
NW = float(INNER * D)
GRPS = [[0, 1, 2, 3], [4, 5, 6, 7]]


class Ledger:

    def __init__(self, sems, engines):
        self.sems = sems
        self.engines = engines
        self.cnt = {e: 0 for e in sems}
        self.mark = {e: {} for e in engines}
        self.buf = {}
        self.queue = {e: [] for e in engines}

    def emit(self, e, fn, reads=(), writes=(), dma=False, chan=None):
        sem_name = chan if (dma and chan) else e
        assert sem_name in self.sems, sem_name
        need = {}
        for k in reads:
            b = self.buf.get(k)
            if b and b["w"] is not None:
                se, sv = b["w"]
                need[se] = max(need.get(se, -1), sv)
        for k in writes:
            b = self.buf.get(k)
            if b:
                if b["w"] is not None:
                    se, sv = b["w"]
                    need[se] = max(need.get(se, -1), sv)
                for se, sv in b["r"].items():
                    need[se] = max(need.get(se, -1), sv)
        for se in sorted(need):
            sv = need[se]
            if se == e == "t":
                continue
            if self.mark[e].get(se, -1) < sv:
                self.queue[e].append(("wait", se, sv))
                self.mark[e][se] = sv
        step = 16 if dma else 1
        self.cnt[sem_name] = self.cnt.get(sem_name, 0) + step
        v = self.cnt[sem_name]
        self.queue[e].append(("op", fn, step, sem_name))
        for k in reads:
            b = self.buf.setdefault(k, {"w": None, "r": {}})
            b["r"][sem_name] = max(b["r"].get(sem_name, -1), v)
        for k in writes:
            self.buf[k] = {"w": (sem_name, v), "r": {}}

    def barrier(self):
        tops = dict(self.cnt)
        for e in self.queue:
            for se, sv in tops.items():
                if se != e and sv > 0 and self.mark[e].get(se, -1) < sv:
                    self.queue[e].append(("wait", se, sv))
                    self.mark[e][se] = sv


def build_program():
    nc = bass.Bass("TRN2", target_bir_lowering=False)

    u_full = nc.dram_tensor("u_full", [T, D], F32, kind="ExternalInput").ap()
    u_tok = nc.dram_tensor("u_tok", [TQ, D], F32, kind="ExternalInput").ap()
    wxdtT_d = nc.dram_tensor("wxdtT", [D, RK], F32, kind="ExternalInput").ap()
    bxdt_d = nc.dram_tensor("bxdt", [RK, 1], F32, kind="ExternalInput").ap()
    wdtT_d = nc.dram_tensor("wdtT_sl", [RK, DSL], F32, kind="ExternalInput").ap()
    bdt_d = nc.dram_tensor("bdt_sl", [DSL, 1], F32, kind="ExternalInput").ap()
    wbT_d = nc.dram_tensor("wbT", [D, NST], F32, kind="ExternalInput").ap()
    bb_d = nc.dram_tensor("bb", [NST, 1], F32, kind="ExternalInput").ap()
    wcT_d = nc.dram_tensor("wcT", [D, NST], F32, kind="ExternalInput").ap()
    bc_d = nc.dram_tensor("bc", [NST, 1], F32, kind="ExternalInput").ap()
    alog_d = nc.dram_tensor("alog_sl", [DSL, NST], F32, kind="ExternalInput").ap()
    dvec_d = nc.dram_tensor("dvec", [1, D], F32, kind="ExternalInput").ap()
    eye_d = nc.dram_tensor("eye", [128, 128], F32, kind="ExternalInput").ap()
    wgT_d = nc.dram_tensor("wgT_q", [D, 1024], F32, kind="ExternalInput").ap()
    wuT_d = nc.dram_tensor("wuT_q", [D, 1024], F32, kind="ExternalInput").ap()
    wdT_d = nc.dram_tensor("wdT_q", [1024, D], F32, kind="ExternalInput").ap()
    out_d = nc.dram_tensor("out", [TQ, D], F32, kind="ExternalOutput").ap()
    bc_B = nc.dram_tensor("bc_B", [NST, T], F32).ap()
    bc_C = nc.dram_tensor("bc_C", [NST, T], F32).ap()
    scal_dr = nc.dram_tensor("scal_dr", [1, 8], F32).ap()
    cc_y_in = [nc.dram_tensor(f"ccyi{k}", [TQ, DSL], F32) for k in range(4)]
    cc_y_out = [nc.dram_tensor(f"ccyo{k}", [4, TQ, DSL], F32) for k in range(4)]
    cc_w_in = nc.dram_tensor("ccwi", [3, 1024, 1024], BF16)
    cc_w_out = nc.dram_tensor("ccwo", [4, 3, 1024, 1024], BF16)
    cc_s_in = nc.dram_tensor("ccsi", [1, 4], F32)
    cc_s_out = nc.dram_tensor("ccso", [1, 4], F32)
    cc_y_sel = nc.dram_tensor("ccysel", [4, TQ, DSL], F32)

    _st = ExitStack()
    with _st:
        eye_sb = _st.enter_context(nc.sbuf_tensor("eye_sb", [128, 128], F32))
        eyeb_sb = _st.enter_context(nc.sbuf_tensor("eyeb_sb", [128, 128], BF16))
        wxdtT_sb = _st.enter_context(nc.sbuf_tensor("wxdtT_sb", [128, 8 * RK], BF16))
        wbT_sb = _st.enter_context(nc.sbuf_tensor("wbT_sb", [128, 8 * NST], BF16))
        wcT_sb = _st.enter_context(nc.sbuf_tensor("wcT_sb", [128, 8 * NST], BF16))
        wdtT_sb = _st.enter_context(nc.sbuf_tensor("wdtT_sb", [RK, DSL], BF16))
        bxdt_sb = _st.enter_context(nc.sbuf_tensor("bxdt_sb", [RK, 1], F32))
        bdt_sb = _st.enter_context(nc.sbuf_tensor("bdt_sb", [128, 2], F32))
        bb_sb = _st.enter_context(nc.sbuf_tensor("bb_sb", [NST, 1], F32))
        bc_sb = _st.enter_context(nc.sbuf_tensor("bc_sb", [NST, 1], F32))
        acol_sb = _st.enter_context(nc.sbuf_tensor("acol_sb", [128, 2 * NST], F32))
        hstate = _st.enter_context(nc.sbuf_tensor("hstate", [128, 2 * NST], F32))
        wsum_sb = _st.enter_context(nc.sbuf_tensor("wsum_sb", [128, 4], F32))
        wrow_sb = _st.enter_context(nc.sbuf_tensor("wrow_sb", [1, 128], F32))
        wsc_sb = _st.enter_context(nc.sbuf_tensor("wsc_sb", [1, 8], F32))
        wscb_sb = _st.enter_context(nc.sbuf_tensor("wscb_sb", [128, 8], F32))
        u_ch = _st.enter_context(nc.sbuf_tensor("u_ch", [128, 2 * D], F32))
        sq_ch = _st.enter_context(nc.sbuf_tensor("sq_ch", [128, D], F32))
        st_ch = _st.enter_context(nc.sbuf_tensor("st_ch", [128, 8], F32))
        xn_ch = _st.enter_context(nc.sbuf_tensor("xn_ch", [128, 2 * D], F32))
        xnT_sb = _st.enter_context(nc.sbuf_tensor("xnT_sb", [128, 8 * Q], BF16))
        xnsl_sb = _st.enter_context(nc.sbuf_tensor("xnsl_sb", [128, 2 * Q], F32))
        xdt_sb = _st.enter_context(nc.sbuf_tensor("xdt_sb", [RK, Q], BF16))
        dt_sb = _st.enter_context(nc.sbuf_tensor("dt_sb", [128, 2 * Q], F32))
        bcq_sb = _st.enter_context(nc.sbuf_tensor("bcq_sb", [NST, Q], F32))
        ccq_sb = _st.enter_context(nc.sbuf_tensor("ccq_sb", [NST, Q], F32))
        c_sb = _st.enter_context(nc.sbuf_tensor("c_sb", [128, 2 * Q], F32))
        bbc_sb = _st.enter_context(nc.sbuf_tensor("bbc_sb", [128, 2 * Q], F32))
        cbc_sb = _st.enter_context(nc.sbuf_tensor("cbc_sb", [128, 2 * Q], F32))
        abar_sb = _st.enter_context(nc.sbuf_tensor("abar_sb", [128, 2 * Q], F32))
        in1_sb = _st.enter_context(nc.sbuf_tensor("in1_sb", [128, 2 * Q], F32))
        h_sb = _st.enter_context(nc.sbuf_tensor("h_sb", [128, 2 * Q], F32))
        hc_sb = _st.enter_context(nc.sbuf_tensor("hc_sb", [128, 2 * Q], F32))
        ych_sb = _st.enter_context(nc.sbuf_tensor("ych_sb", [128, 2 * Q], F32))
        ytok_sb = _st.enter_context(nc.sbuf_tensor("ytok_sb", [128, 4 * DSL], F32))
        x2_sb = _st.enter_context(nc.sbuf_tensor("x2_sb", [128, NTT * D], F32))
        xn2_sb = _st.enter_context(nc.sbuf_tensor("xn2_sb", [128, 2048], F32))
        fst_sb = _st.enter_context(nc.sbuf_tensor("fst_sb", [128, 16], F32))
        xq_sb = _st.enter_context(nc.sbuf_tensor("xq_sb", [128, NTT * D], BF16))
        xqT_sb = _st.enter_context(nc.sbuf_tensor("xqT_sb", [128, 8 * FBL], BF16))
        wA_sb = _st.enter_context(nc.sbuf_tensor("wA_sb", [128, 2 * 8 * 512], BF16))
        wB_sb = _st.enter_context(nc.sbuf_tensor("wB_sb", [128, 2 * 8 * 512], BF16))
        gate_sb = _st.enter_context(nc.sbuf_tensor("gate_sb", [128, 512], F32))
        arena_a = _st.enter_context(nc.sbuf_tensor("arena_a", [128, 8192], F32))
        arena_b = _st.enter_context(nc.sbuf_tensor("arena_b", [128, 8192], F32))
        ot_sb = _st.enter_context(nc.sbuf_tensor("ot_sb", [128, D], F32))
        g_sb = arena_a.ap()
        wq_ld = arena_a.ap()
        gq_sb = arena_b.ap().bitcast(BF16)[:, 0:NTT * INNER]
        gqT_sb = arena_b.ap().bitcast(BF16)[:, NTT * INNER:2 * NTT * INNER]
        wq_r = arena_b.ap()
        wq_b = xn2_sb.ap().bitcast(BF16)
        yg_sb = u_ch.ap()
        uf_sb = xn_ch.ap()
        edt_sb = sq_ch.ap()[:, 0:Q]
        fsq_sb = sq_ch.ap()
        fdu_sb = xnsl_sb.ap()
        dbc_sb = bbc_sb.ap()[:, 0:D]
        ps0 = _st.enter_context(nc.psum_tensor("ps0", [128, 512], F32))
        ps1 = _st.enter_context(nc.psum_tensor("ps1", [128, 512], F32))
        ps2 = _st.enter_context(nc.psum_tensor("ps2", [128, 512], F32))
        ps3 = _st.enter_context(nc.psum_tensor("ps3", [128, 512], F32))
        ps4 = _st.enter_context(nc.psum_tensor("ps4", [128, 512], F32))
        ps5 = _st.enter_context(nc.psum_tensor("ps5", [128, 512], F32))
        ps6 = _st.enter_context(nc.psum_tensor("ps6", [128, 512], F32))
        ps7 = _st.enter_context(nc.psum_tensor("ps7", [128, 512], F32))
        sem_v = _st.enter_context(nc.semaphore("sem_v"))
        sem_s = _st.enter_context(nc.semaphore("sem_s"))
        sem_t = _st.enter_context(nc.semaphore("sem_t"))
        sem_g = _st.enter_context(nc.semaphore("sem_g"))
        sem_q = _st.enter_context(nc.semaphore("sem_q"))
        block = _st.enter_context(nc.Block())
        sems = {"v": sem_v, "s": sem_s, "t": sem_t, "g": sem_g, "q": sem_q}
        CHANS = ["qm", "qwl", "qwr", "qss", "qsc", "qsw", "qu0", "qu1",
                 "qsB", "qsC", "qb0", "qb1", "qc0", "qc1",
                 "qsy0", "qsy1", "qsy2", "qsy3", "qdbg", "qd", "qsel",
                 "qy0", "qy1", "qf0", "qf1", "qw0", "qw1", "qw20", "qw21", "qso"]
        for cn in CHANS:
            sems[cn] = _st.enter_context(nc.semaphore("sem_" + cn))
        L = Ledger(sems, ["v", "s", "t", "g", "q"])
        E = L.emit
        PID = [None]

        E("q", lambda q: q.dma_start(out=eye_sb[:, :], in_=eye_d[:, :]),
          writes=["eye"], dma=True, chan="qm")
        E("v", lambda v: v.tensor_copy(eyeb_sb[:, :], eye_sb[:, :]),
          reads=["eye"], writes=["eyeb"])
        for nm, dst, src, nk, w in [
            ("wxdtT", wxdtT_sb, wxdtT_d, 8, RK),
            ("wbT", wbT_sb, wbT_d, 8, NST),
            ("wcT", wcT_sb, wcT_d, 8, NST),
        ]:
            E("q", lambda q, src=src, nk=nk, w=w: q.dma_start(
                out=wq_ld[:, 0:nk * w].rearrange("p (k r) -> p k r", k=nk),
                in_=src.rearrange("(k p) r -> p k r", p=128)),
              writes=["wq_ld"], dma=True, chan="qwl")
            E("v", lambda v, dst=dst, nk=nk, w=w: v.tensor_copy(
                dst[:, :], wq_ld[:, 0:nk * w]),
              reads=["wq_ld"], writes=[nm])
        E("q", lambda q: q.dma_start(out=wq_ld[0:RK, 0:DSL], in_=wdtT_d[:, :]),
          writes=["wq_ld"], dma=True, chan="qwl")
        E("v", lambda v: v.tensor_copy(wdtT_sb[:, :], wq_ld[0:RK, 0:DSL]),
          reads=["wq_ld"], writes=["wdtT"])
        E("q", lambda q: q.dma_start(out=bxdt_sb[:, :], in_=bxdt_d[:, :]),
          writes=["bxdt"], dma=True, chan="qm")
        E("q", lambda q: q.dma_start(
            out=bdt_sb[:, :].rearrange("p (d c) -> p d c", d=2),
            in_=bdt_d.rearrange("(d p) c -> p d c", p=128),
            allow_slow_non_contiguous=True),
          writes=["bdt"], dma=True, chan="qm")
        E("q", lambda q: q.dma_start(out=bb_sb[:, :], in_=bb_d[:, :]),
          writes=["bbias"], dma=True, chan="qm")
        E("q", lambda q: q.dma_start(out=bc_sb[:, :], in_=bc_d[:, :]),
          writes=["cbias"], dma=True, chan="qm")
        E("q", lambda q: q.dma_start(
            out=acol_sb[:, :].rearrange("p (d n) -> p d n", d=2),
            in_=alog_d.rearrange("(d p) n -> p d n", p=128)),
          writes=["acol"], dma=True, chan="qm")
        E("s", lambda s: s.activation(acol_sb[:, :], acol_sb[:, :], AF.Exp),
          reads=["acol"], writes=["acol"])
        E("v", lambda v: v.tensor_scalar_mul(acol_sb[:, :], acol_sb[:, :], -1.0),
          reads=["acol"], writes=["acol"])
        E("v", lambda v: v.memset(hstate[:, :], 0.0),
          writes=[f"hst{i}" for i in range(2 * NST)])

        wsrc = [wgT_d, wuT_d, wdT_d]
        for m in range(3):
            E("q", lambda q, m=m: q.dma_start(
                out=wq_ld[:, :].rearrange("p (k c) -> p k c", k=8),
                in_=wsrc[m].rearrange("(k p) c -> p k c", p=128)),
              writes=["wq_ld"], dma=True, chan="qwl")
            E("s", lambda s, m=m: s.activation(
                wq_r[:, :], wq_ld[:, :], AF.Abs,
                accum_out=wsum_sb[:, m:m + 1]),
              reads=["wq_ld"], writes=["wq_r", f"wsum{m}"])
        for m in range(3):
            E("q", lambda q, m=m: q.dma_start(
                out=wrow_sb[:, :], in_=wsum_sb[:, m:m + 1],
                allow_slow_non_contiguous=True),
              reads=[f"wsum{m}"], writes=["wrow"], dma=True, chan="qwr")
            E("v", lambda v, m=m: v.reduce_sum(
                wsc_sb[0:1, m:m + 1], wrow_sb[:, :], axis=mybir.AxisListType.X),
              reads=["wrow"], writes=["wsc"])
        E("q", lambda q: q.dma_start(out=cc_s_in[0:1, :], in_=wsc_sb[0:1, 0:4]),
          reads=["wsc"], writes=["ccsi"], dma=True, chan="qss")
        E("g", lambda g: g.collective_compute(
            "AllReduce", ALU.add, replica_groups=GRPS,
            ins=[cc_s_in.ap().opt()], outs=[cc_s_out.ap().opt()]),
          reads=["ccsi"], writes=["ccso"])
        E("q", lambda q: q.dma_start(out=wsc_sb[0:1, 0:4], in_=cc_s_out[0:1, :]),
          reads=["ccso"], writes=["wsc"], dma=True, chan="qss")
        E("v", lambda v: v.tensor_scalar(
            wsc_sb[0:1, 4:8], wsc_sb[0:1, 0:4], 1.0 / NW, 1e-5,
            op0=ALU.mult, op1=ALU.max),
          reads=["wsc"], writes=["wsc"])
        E("v", lambda v: v.reciprocal(wsc_sb[0:1, 0:4], wsc_sb[0:1, 4:8]),
          reads=["wsc"], writes=["wsc"])
        E("q", lambda q: q.dma_start(out=scal_dr[0:1, :], in_=wsc_sb[0:1, :]),
          reads=["wsc"], writes=["scal"], dma=True, chan="qsc")
        E("q", lambda q: q.dma_start(out=wscb_sb[:, :],
                                     in_=scal_dr.broadcast_to([128, 8])),
          reads=["scal"], writes=["wscb"], dma=True, chan="qsc")
        for m in range(3):
            E("q", lambda q, m=m: q.dma_start(
                out=wq_ld[:, :].rearrange("p (k c) -> p k c", k=8),
                in_=wsrc[m].rearrange("(k p) c -> p k c", p=128)),
              writes=["wq_ld"], dma=True, chan="qwl")
            E("v", lambda v, m=m: v.tensor_scalar(
                wq_r[:, :], wq_ld[:, :], wscb_sb[:, m:m + 1], MAGIC,
                op0=ALU.mult, op1=ALU.add),
              reads=["wq_ld", "wscb"], writes=["wq_r"])
            E("v", lambda v: v.tensor_scalar(
                wq_r[:, :], wq_r[:, :], MAGIC, None, op0=ALU.subtract),
              reads=["wq_r"], writes=["wq_r"])
            for hf in range(2):
                E("v", lambda v, hf=hf: v.tensor_scalar(
                    wq_b[:, 0:4096], wq_r[:, hf * 4096:(hf + 1) * 4096],
                    1.0, -1.0, op0=ALU.min, op1=ALU.max),
                  reads=["wq_r"], writes=["wq_b"])
                E("q", lambda q, m=m, hf=hf: q.dma_start(
                    out=cc_w_in[m, :, :].rearrange("(k p) c -> p k c", p=128)
                    [:, hf * 4:(hf + 1) * 4, :],
                    in_=wq_b[:, 0:4096].rearrange("p (k c) -> p k c", k=4)),
                  reads=["wq_b"], writes=["ccwi"], dma=True, chan="qsw")
        E("g", lambda g: g.collective_compute(
            "AllGather", ALU.bypass, replica_groups=GRPS,
            ins=[cc_w_in.ap().opt()], outs=[cc_w_out.ap().opt()]),
          reads=["ccwi"], writes=["ccwo"])

        psT = [ps3, ps4]

        def ln_apply(src_key, src_tile, st, dst_key, dst_tile):
            E("v", lambda v: v.reduce_sum(
                st[:, 0:1], src_tile, axis=mybir.AxisListType.X),
              reads=[src_key], writes=["stm"])
            E("s", lambda s: s.activation(
                sq_ch[:, :], src_tile, AF.Square, accum_out=st[:, 2:3]),
              reads=[src_key], writes=["stsq", "sts2"])
            E("v", lambda v: v.tensor_scalar(
                st[:, 0:1], st[:, 0:1], 1.0 / D, None, op0=ALU.mult),
              reads=["stm"], writes=["stm"])
            E("v", lambda v: v.tensor_scalar(
                st[:, 2:3], st[:, 2:3], 1.0 / D, EPS,
                op0=ALU.mult, op1=ALU.add),
              reads=["sts2"], writes=["sts2"])
            E("v", lambda v: v.tensor_tensor(
                st[:, 3:4], st[:, 0:1], st[:, 0:1], op=ALU.mult),
              reads=["stm"], writes=["stm2"])
            E("v", lambda v: v.tensor_tensor(
                st[:, 1:2], st[:, 2:3], st[:, 3:4], op=ALU.subtract),
              reads=["sts2", "stm2"], writes=["str"])
            E("s", lambda s: s.activation(st[:, 1:2], st[:, 1:2], AF.Ln),
              reads=["str"], writes=["str"])
            E("s", lambda s: s.activation(st[:, 1:2], st[:, 1:2], AF.Exp,
                                          scale=-0.5),
              reads=["str"], writes=["str"])
            E("v", lambda v: v.tensor_scalar(
                dst_tile, src_tile, st[:, 0:1], st[:, 1:2],
                op0=ALU.subtract, op1=ALU.mult),
              reads=[src_key, "stm", "str"], writes=[dst_key])

        for ch in range(NCH):
            tok0 = ch * Q
            kq = ch // (NCH // 4)
            for tt in range(4):
                us = tt % 2
                ut = u_ch[:, us * D:(us + 1) * D]
                xt_ = xn_ch[:, us * D:(us + 1) * D]
                E("q", lambda q, tok0=tok0, tt=tt, ut=ut: q.dma_start(
                    out=ut, in_=u_full[tok0 + tt * 128: tok0 + (tt + 1) * 128, :]),
                  writes=[f"u{us}"], dma=True, chan=f"qu{us}")
                ln_apply(f"u{us}", ut, st_ch, f"xn{us}", xt_)
                for k in range(8):
                    pst = psT[k % 2]
                    E("t", lambda t, xt_=xt_, k=k, pst=pst: t.transpose(
                        pst[:, 0:128], xt_[:, k * 128:(k + 1) * 128],
                        eye_sb[:, :]),
                      reads=[f"xn{us}", "eye"], writes=[f"psT{k % 2}"])
                    E("s", lambda s, k=k, tt=tt, pst=pst: s.activation(
                        xnT_sb[:, k * Q + tt * 128: k * Q + (tt + 1) * 128],
                        pst[:, 0:128], AF.Copy),
                      reads=[f"psT{k % 2}"], writes=[f"xnT{k}"])
                    if k < 2:
                        E("s", lambda s, k=k, tt=tt, pst=pst: s.activation(
                            xnsl_sb[:, k * Q + tt * 128: k * Q + (tt + 1) * 128],
                            pst[:, 0:128], AF.Copy),
                          reads=[f"psT{k % 2}"], writes=[f"xnsl{k}"])
            for k in range(8):
                E("t", lambda t, k=k: t.matmul(
                    ps0[0:RK, 0:Q], wxdtT_sb[:, k * RK:(k + 1) * RK],
                    xnT_sb[:, k * Q:(k + 1) * Q],
                    start=(k == 0), stop=(k == 7)),
                  reads=[f"xnT{k}", "wxdtT"], writes=["ps0"])
            E("s", lambda s: s.activation(
                xdt_sb[:, :], ps0[0:RK, 0:Q], AF.Identity, bias=bxdt_sb[:, :]),
              reads=["ps0", "bxdt"], writes=["xdt"])
            for k in range(8):
                E("t", lambda t, k=k: t.matmul(
                    ps0[0:NST, 0:Q], wbT_sb[:, k * NST:(k + 1) * NST],
                    xnT_sb[:, k * Q:(k + 1) * Q],
                    start=(k == 0), stop=(k == 7)),
                  reads=[f"xnT{k}", "wbT"], writes=["ps0"])
            E("s", lambda s: s.activation(
                bcq_sb[:, :], ps0[0:NST, 0:Q], AF.Identity, bias=bb_sb[:, :]),
              reads=["ps0", "bbias"], writes=["bcq"])
            E("q", lambda q, tok0=tok0: q.dma_start(
                out=bc_B[:, tok0:tok0 + Q], in_=bcq_sb[:, :]),
              reads=["bcq"], writes=["bcBdr"], dma=True, chan="qsB")
            for k in range(8):
                E("t", lambda t, k=k: t.matmul(
                    ps0[0:NST, 0:Q], wcT_sb[:, k * NST:(k + 1) * NST],
                    xnT_sb[:, k * Q:(k + 1) * Q],
                    start=(k == 0), stop=(k == 7)),
                  reads=[f"xnT{k}", "wcT"], writes=["ps0"])
            E("s", lambda s: s.activation(
                ccq_sb[:, :], ps0[0:NST, 0:Q], AF.Identity, bias=bc_sb[:, :]),
              reads=["ps0", "cbias"], writes=["ccq"])
            E("q", lambda q, tok0=tok0: q.dma_start(
                out=bc_C[:, tok0:tok0 + Q], in_=ccq_sb[:, :]),
              reads=["ccq"], writes=["bcCdr"], dma=True, chan="qsC")
            for dti in range(2):
                psd = [ps1, ps2][dti]
                E("t", lambda t, dti=dti, psd=psd: t.matmul(
                    psd[:, 0:Q], wdtT_sb[:, dti * 128:(dti + 1) * 128],
                    xdt_sb[:, :], start=True, stop=True),
                  reads=["xdt", "wdtT"], writes=[f"psd{dti}"])
                E("s", lambda s, dti=dti, psd=psd: s.activation(
                    edt_sb[:, :], psd[:, 0:Q], AF.Exp,
                    bias=bdt_sb[:, dti:dti + 1]),
                  reads=[f"psd{dti}", "bdt"], writes=["stsq"])
                E("s", lambda s, dti=dti: s.activation(
                    dt_sb[:, dti * Q:(dti + 1) * Q], edt_sb[:, :], AF.Ln,
                    bias=1.0),
                  reads=["stsq"], writes=[f"dt{dti}"])
                E("g", lambda g, dti=dti: g.tensor_tensor(
                    c_sb[:, dti * Q:(dti + 1) * Q],
                    dt_sb[:, dti * Q:(dti + 1) * Q],
                    xnsl_sb[:, dti * Q:(dti + 1) * Q], op=ALU.mult),
                  reads=[f"dt{dti}", f"xnsl{dti}"], writes=[f"c{dti}"])
            for n in range(NST):
                bsl = n % 2
                E("q", lambda q, n=n, tok0=tok0, bsl=bsl: q.dma_start(
                    out=bbc_sb[:, bsl * Q:(bsl + 1) * Q],
                    in_=bc_B[n:n + 1, tok0:tok0 + Q].broadcast_to([128, Q])),
                  reads=["bcBdr"], writes=[f"bbc{bsl}"], dma=True, chan=f"qb{bsl}")
                E("q", lambda q, n=n, tok0=tok0, bsl=bsl: q.dma_start(
                    out=cbc_sb[:, bsl * Q:(bsl + 1) * Q],
                    in_=bc_C[n:n + 1, tok0:tok0 + Q].broadcast_to([128, Q])),
                  reads=["bcCdr"], writes=[f"cbc{bsl}"], dma=True, chan=f"qc{bsl}")
                for dti in range(2):
                    asl = (2 * n + dti) % 2
                    col = dti * NST + n
                    E("s", lambda s, n=n, dti=dti, asl=asl: s.activation(
                        abar_sb[:, asl * Q:(asl + 1) * Q],
                        dt_sb[:, dti * Q:(dti + 1) * Q], AF.Exp,
                        scale=acol_sb[:, dti * NST + n:dti * NST + n + 1]),
                      reads=[f"dt{dti}", "acol"], writes=[f"abar{asl}"])
                    E("g", lambda g, dti=dti, asl=asl, bsl=bsl: g.tensor_tensor(
                        in1_sb[:, asl * Q:(asl + 1) * Q],
                        c_sb[:, dti * Q:(dti + 1) * Q],
                        bbc_sb[:, bsl * Q:(bsl + 1) * Q], op=ALU.mult),
                      reads=[f"c{dti}", f"bbc{bsl}"], writes=[f"in1{asl}"])
                    E("v", lambda v, asl=asl, col=col: v.tensor_tensor_scan(
                        h_sb[:, asl * Q:(asl + 1) * Q],
                        abar_sb[:, asl * Q:(asl + 1) * Q],
                        in1_sb[:, asl * Q:(asl + 1) * Q],
                        hstate[:, col:col + 1], op0=ALU.mult, op1=ALU.add),
                      reads=[f"abar{asl}", f"in1{asl}", f"hst{col}"],
                      writes=[f"h{asl}"])
                    E("s", lambda s, asl=asl, col=col: s.activation(
                        hstate[:, col:col + 1],
                        h_sb[:, asl * Q + Q - 1: asl * Q + Q], AF.Copy),
                      reads=[f"h{asl}"], writes=[f"hst{col}"])
                    if n == 0:
                        E("v", lambda v, asl=asl, bsl=bsl, dti=dti:
                          v.tensor_tensor(
                              ych_sb[:, dti * Q:(dti + 1) * Q],
                              h_sb[:, asl * Q:(asl + 1) * Q],
                              cbc_sb[:, bsl * Q:(bsl + 1) * Q], op=ALU.mult),
                          reads=[f"h{asl}", f"cbc{bsl}"], writes=[f"ych{dti}"])
                    else:
                        heng = "v" if (n % 2 == 0) else "g"
                        E(heng, lambda e, asl=asl, bsl=bsl, hsl=n % 2:
                          e.tensor_tensor(
                              hc_sb[:, hsl * Q:(hsl + 1) * Q],
                              h_sb[:, asl * Q:(asl + 1) * Q],
                              cbc_sb[:, bsl * Q:(bsl + 1) * Q], op=ALU.mult),
                          reads=[f"h{asl}", f"cbc{bsl}"], writes=[f"hc{n % 2}"])
                        E(heng, lambda e, dti=dti, hsl=n % 2: e.tensor_tensor(
                            ych_sb[:, dti * Q:(dti + 1) * Q],
                            ych_sb[:, dti * Q:(dti + 1) * Q],
                            hc_sb[:, hsl * Q:(hsl + 1) * Q],
                            op=ALU.add),
                          reads=[f"hc{n % 2}", f"ych{dti}"], writes=[f"ych{dti}"])
            for tt in range(4):
                for dti in range(2):
                    pst = psT[dti]
                    E("t", lambda t, dti=dti, tt=tt, pst=pst: t.transpose(
                        pst[:, 0:128],
                        ych_sb[:, dti * Q + tt * 128: dti * Q + (tt + 1) * 128],
                        eye_sb[:, :]),
                      reads=[f"ych{dti}", "eye"], writes=[f"psT{dti}"])
                    E("s", lambda s, dti=dti, tt=tt, pst=pst: s.activation(
                        ytok_sb[:, tt * DSL + dti * 128:
                                tt * DSL + (dti + 1) * 128],
                        pst[:, 0:128], AF.Copy),
                      reads=[f"psT{dti}"], writes=[f"ytok{tt}"])
                E("q", lambda q, tt=tt, tok0=tok0, kq=kq: q.dma_start(
                    out=cc_y_in[kq][(tok0 % TQ) + tt * 128:
                                    (tok0 % TQ) + (tt + 1) * 128, :],
                    in_=ytok_sb[:, tt * DSL:(tt + 1) * DSL]),
                  reads=[f"ytok{tt}"], writes=[f"ccy{kq}"], dma=True, chan=f"qsy{kq}")
            if ch % (NCH // 4) == (NCH // 4) - 1:
                E("g", lambda g, kq=kq: g.collective_compute(
                    "AllGather", ALU.bypass, replica_groups=GRPS,
                    ins=[cc_y_in[kq].ap().opt()],
                    outs=[cc_y_out[kq].ap().opt()]),
                  reads=[f"ccy{kq}"], writes=[f"ccyog{kq}"])

        L.barrier()

        E("q", lambda q: q.dma_start(out=dbc_sb[:, :],
                                     in_=dvec_d.broadcast_to([128, D])),
          writes=["dbc"], dma=True, chan="qd")
        for k in range(4):
            E("q", lambda q, k=k: q.dma_start(
                out=cc_y_sel.ap(), in_=cc_y_out[k].ap(), cond=PID[0][k]),
              reads=[f"ccyog{k}"], writes=["ccysel"], dma=True, chan="qsel")
        def ln2_apply(src_key, src_tile, st, dst_key, dst_tile, scr):
            E("v", lambda v: v.reduce_sum(
                st[:, 0:1], src_tile, axis=mybir.AxisListType.X),
              reads=[src_key], writes=["fm"])
            E("s", lambda s: s.activation(
                scr[:, :], src_tile, AF.Square, accum_out=st[:, 2:3]),
              reads=[src_key], writes=["fdu", "fs2"])
            E("v", lambda v: v.tensor_scalar(
                st[:, 0:1], st[:, 0:1], 1.0 / D, None, op0=ALU.mult),
              reads=["fm"], writes=["fm"])
            E("v", lambda v: v.tensor_scalar(
                st[:, 2:3], st[:, 2:3], 1.0 / D, EPS, op0=ALU.mult, op1=ALU.add),
              reads=["fs2"], writes=["fs2"])
            E("v", lambda v: v.tensor_tensor(
                st[:, 3:4], st[:, 0:1], st[:, 0:1], op=ALU.mult),
              reads=["fm"], writes=["fm2"])
            E("v", lambda v: v.tensor_tensor(
                st[:, 1:2], st[:, 2:3], st[:, 3:4], op=ALU.subtract),
              reads=["fs2", "fm2"], writes=["fr"])
            E("s", lambda s: s.activation(st[:, 1:2], st[:, 1:2], AF.Ln),
              reads=["fr"], writes=["fr"])
            E("s", lambda s: s.activation(st[:, 1:2], st[:, 1:2], AF.Exp,
                                          scale=-0.5),
              reads=["fr"], writes=["fr"])
            E("v", lambda v: v.tensor_scalar(
                dst_tile, src_tile, st[:, 0:1], st[:, 1:2],
                op0=ALU.subtract, op1=ALU.mult),
              reads=[src_key, "fm", "fr"], writes=[dst_key])

        psG = [ps1, ps5]
        psU = [ps2, ps6]
        psD_ = [ps7, ps0]

        for bl in range(NFB):
            fb0 = bl * FBL
            for tt in range(NTT):
                r0 = fb0 + tt * 128
                E("q", lambda q, r0=r0, tt=tt: q.dma_start(
                    out=yg_sb[:, tt * D:(tt + 1) * D]
                    .rearrange("p (j c) -> p j c", j=4),
                    in_=cc_y_sel.ap()[:, r0:r0 + 128, :]
                    .transpose([1, 0, 2])),
                  reads=["ccysel"], writes=[f"yg{tt}"], dma=True, chan=f"qy{tt}")
                E("q", lambda q, r0=r0, tt=tt: q.dma_start(
                    out=uf_sb[:, tt * D:(tt + 1) * D],
                    in_=u_tok[r0:r0 + 128, :]),
                  writes=[f"uf{tt}"], dma=True, chan=f"qf{tt}")
                uft = uf_sb[:, tt * D:(tt + 1) * D]
                ln2_apply(f"uf{tt}", uft, fst_sb, "fsq", fsq_sb[:, :],
                          fdu_sb)
                E("v", lambda v: v.tensor_tensor(
                    fdu_sb[:, :], fsq_sb[:, :], dbc_sb[:, :], op=ALU.mult),
                  reads=["fsq", "dbc"], writes=["fdu"])
                E("v", lambda v, tt=tt, uft=uft: v.tensor_tensor(
                    x2_sb[:, tt * D:(tt + 1) * D], uft,
                    yg_sb[:, tt * D:(tt + 1) * D], op=ALU.add),
                  reads=[f"uf{tt}", f"yg{tt}"], writes=[f"x2{tt}"])
                E("v", lambda v, tt=tt: v.tensor_tensor(
                    x2_sb[:, tt * D:(tt + 1) * D],
                    x2_sb[:, tt * D:(tt + 1) * D], fdu_sb[:, :], op=ALU.add),
                  reads=["fdu", f"x2{tt}"], writes=[f"x2{tt}"])
                ln2_apply(f"x2{tt}", x2_sb[:, tt * D:(tt + 1) * D], fst_sb,
                          "fsq", fsq_sb[:, :], fdu_sb)
                E("v", lambda v: v.reduce_max(
                    fst_sb[:, 4:5], fsq_sb[:, :], axis=mybir.AxisListType.X,
                    apply_absolute_value=True),
                  reads=["fsq"], writes=["fmax"])
                E("v", lambda v: v.tensor_scalar_max(
                    fst_sb[:, 4:5], fst_sb[:, 4:5], 1e-5),
                  reads=["fmax"], writes=["fmax"])
                E("v", lambda v: v.reciprocal(fst_sb[:, 5:6], fst_sb[:, 4:5]),
                  reads=["fmax"], writes=["fsx"])
                E("v", lambda v: v.tensor_scalar(
                    fst_sb[:, 5:6], fst_sb[:, 5:6], 127.0, None, op0=ALU.mult),
                  reads=["fsx"], writes=["fsx"])
                E("v", lambda v, tt=tt: v.tensor_scalar(
                    fst_sb[:, 6 + tt:7 + tt], fst_sb[:, 4:5], 1.0 / 127.0, None,
                    op0=ALU.mult),
                  reads=["fmax"], writes=[f"fisx{tt}"])
                E("v", lambda v: v.tensor_scalar(
                    fdu_sb[:, :], fsq_sb[:, :], fst_sb[:, 5:6], MAGIC,
                    op0=ALU.mult, op1=ALU.add),
                  reads=["fsq", "fsx"], writes=["fdu"])
                E("v", lambda v, tt=tt: v.tensor_scalar(
                    xq_sb[:, tt * D:(tt + 1) * D], fdu_sb[:, :], MAGIC, None,
                    op0=ALU.subtract),
                  reads=["fdu"], writes=[f"xq{tt}"])
                for k in range(8):
                    pst = psT[k % 2]
                    E("t", lambda t, k=k, tt=tt, pst=pst: t.transpose(
                        pst.bitcast(BF16)[:, 0:128],
                        xq_sb[:, tt * D + k * 128: tt * D + (k + 1) * 128],
                        eyeb_sb[:, :]),
                      reads=[f"xq{tt}", "eyeb"], writes=[f"psT{k % 2}"])
                    E("s", lambda s, k=k, tt=tt, pst=pst: s.activation(
                        xqT_sb[:, k * FBL + tt * 128: k * FBL + (tt + 1) * 128],
                        pst.bitcast(BF16)[:, 0:128], AF.Copy),
                      reads=[f"psT{k % 2}"], writes=[f"xqT{k}"])
                E("v", lambda v, tt=tt: v.tensor_tensor(
                    fst_sb[:, 8 + tt:9 + tt], fst_sb[:, 6 + tt:7 + tt],
                    wscb_sb[:, 4:5], op=ALU.mult),
                  reads=[f"fisx{tt}", "wscb"], writes=[f"sgv{tt}"])
                E("v", lambda v, tt=tt: v.tensor_tensor(
                    fst_sb[:, 10 + tt:11 + tt], fst_sb[:, 6 + tt:7 + tt],
                    wscb_sb[:, 5:6], op=ALU.mult),
                  reads=[f"fisx{tt}", "wscb"], writes=[f"suv{tt}"])
            for ic in range(8):
                sl_ = ic % 2
                W0 = sl_ * 8 * 512
                E("q", lambda q, ic=ic, W0=W0: q.dma_start(
                    out=wA_sb[:, W0:W0 + 8 * 512].rearrange("p (k c) -> p k c", k=8),
                    in_=cc_w_out[ic // 2, 0, :, (ic % 2) * 512:(ic % 2 + 1) * 512]
                    .rearrange("(k p) c -> p k c", p=128)),
                  reads=["ccwo"], writes=[f"wA{sl_}"], dma=True, chan=f"qw{sl_}")
                E("q", lambda q, ic=ic, W0=W0: q.dma_start(
                    out=wB_sb[:, W0:W0 + 8 * 512].rearrange("p (k c) -> p k c", k=8),
                    in_=cc_w_out[ic // 2, 1, :, (ic % 2) * 512:(ic % 2 + 1) * 512]
                    .rearrange("(k p) c -> p k c", p=128)),
                  reads=["ccwo"], writes=[f"wB{sl_}"], dma=True, chan=f"qw2{sl_}")
                for tt in range(NTT):
                    for k in range(8):
                        E("t", lambda t, k=k, tt=tt, W0=W0: t.matmul(
                            psG[tt][:, :],
                            xqT_sb[:, k * FBL + tt * 128:
                                   k * FBL + (tt + 1) * 128],
                            wA_sb[:, W0 + k * 512:W0 + (k + 1) * 512],
                            start=(k == 0), stop=(k == 7)),
                          reads=[f"xqT{k}", f"wA{sl_}"], writes=[f"psG{tt}"])
                    for k in range(8):
                        E("t", lambda t, k=k, tt=tt, W0=W0: t.matmul(
                            psU[tt][:, :],
                            xqT_sb[:, k * FBL + tt * 128:
                                   k * FBL + (tt + 1) * 128],
                            wB_sb[:, W0 + k * 512:W0 + (k + 1) * 512],
                            start=(k == 0), stop=(k == 7)),
                          reads=[f"xqT{k}", f"wB{sl_}"], writes=[f"psU{tt}"])
                    E("s", lambda s, tt=tt: s.activation(
                        gate_sb[:, :], psG[tt][:, :], AF.Silu,
                        scale=fst_sb[:, 8 + tt:9 + tt]),
                      reads=[f"psG{tt}", f"sgv{tt}"], writes=["gate"])
                    E("v", lambda v, tt=tt, ic=ic: v.scalar_tensor_tensor(
                        g_sb[:, tt * INNER + ic * 512:
                             tt * INNER + (ic + 1) * 512],
                        psU[tt][:, :], fst_sb[:, 10 + tt:11 + tt],
                        gate_sb[:, :], op0=ALU.mult, op1=ALU.mult),
                      reads=[f"psU{tt}", f"suv{tt}", "gate"],
                      writes=[f"g{tt}_{ic}"])
            for tt in range(NTT):
                E("v", lambda v, tt=tt: v.reduce_max(
                    fst_sb[:, 12 + tt:13 + tt],
                    g_sb[:, tt * INNER:(tt + 1) * INNER],
                    axis=mybir.AxisListType.X, apply_absolute_value=True),
                  reads=[f"g{tt}_{ic}" for ic in range(8)], writes=[f"gmax{tt}"])
                E("v", lambda v, tt=tt: v.tensor_scalar_max(
                    fst_sb[:, 12 + tt:13 + tt], fst_sb[:, 12 + tt:13 + tt],
                    1e-5),
                  reads=[f"gmax{tt}"], writes=[f"gmax{tt}"])
                E("v", lambda v, tt=tt: v.reciprocal(
                    fst_sb[:, 14 + tt:15 + tt], fst_sb[:, 12 + tt:13 + tt]),
                  reads=[f"gmax{tt}"], writes=[f"gsx{tt}"])
                E("v", lambda v, tt=tt: v.tensor_scalar(
                    fst_sb[:, 14 + tt:15 + tt], fst_sb[:, 14 + tt:15 + tt],
                    127.0, None, op0=ALU.mult),
                  reads=[f"gsx{tt}"], writes=[f"gsx{tt}"])
                E("v", lambda v, tt=tt: v.tensor_scalar(
                    fst_sb[:, 12 + tt:13 + tt], fst_sb[:, 12 + tt:13 + tt],
                    1.0 / 127.0, None, op0=ALU.mult),
                  reads=[f"gmax{tt}"], writes=[f"gisx{tt}"])
                E("v", lambda v, tt=tt: v.tensor_tensor(
                    fst_sb[:, 12 + tt:13 + tt], fst_sb[:, 12 + tt:13 + tt],
                    wscb_sb[:, 6:7], op=ALU.mult),
                  reads=[f"gisx{tt}", "wscb"], writes=[f"sdv{tt}"])
                for hf in range(2):
                    E("v", lambda v, tt=tt, hf=hf: v.tensor_scalar(
                        xn2_sb[:, :],
                        g_sb[:, tt * INNER + hf * 2048:
                             tt * INNER + (hf + 1) * 2048],
                        fst_sb[:, 14 + tt:15 + tt], MAGIC,
                        op0=ALU.mult, op1=ALU.add),
                      reads=[f"g{tt}_{ic}" for ic in range(8)] + [f"gsx{tt}"],
                      writes=["gqr"])
                    E("v", lambda v, tt=tt, hf=hf: v.tensor_scalar(
                        gq_sb[:, tt * INNER + hf * 2048:
                              tt * INNER + (hf + 1) * 2048],
                        xn2_sb[:, :], MAGIC, None, op0=ALU.subtract),
                      reads=["gqr"], writes=[f"gq{tt}"])
                for k in range(32):
                    pst = psT[k % 2]
                    E("t", lambda t, k=k, tt=tt, pst=pst: t.transpose(
                        pst.bitcast(BF16)[:, 0:128],
                        gq_sb[:, tt * INNER + k * 128:
                              tt * INNER + (k + 1) * 128],
                        eyeb_sb[:, :]),
                      reads=[f"gq{tt}", "eyeb"], writes=[f"psT{k % 2}"])
                    E("s", lambda s, k=k, tt=tt, pst=pst: s.activation(
                        gqT_sb[:, k * FBL + tt * 128: k * FBL + (tt + 1) * 128],
                        pst.bitcast(BF16)[:, 0:128], AF.Copy),
                      reads=[f"psT{k % 2}"], writes=[f"gqT{k}_{tt}"])
            for dh in range(2):
                for kg in range(4):
                    sl_ = kg % 2
                    W0 = sl_ * 8 * 512
                    E("q", lambda q, dh=dh, kg=kg, W0=W0: q.dma_start(
                        out=wA_sb[:, W0:W0 + 8 * 512].rearrange("p (k c) -> p k c", k=8),
                        in_=cc_w_out[kg, 2, :, dh * 512:(dh + 1) * 512]
                        .rearrange("(k p) c -> p k c", p=128)),
                      reads=["ccwo"], writes=[f"wA{sl_}"], dma=True, chan=f"qw{sl_}")
                    for tt in range(NTT):
                        for k8 in range(8):
                            k = kg * 8 + k8
                            E("t", lambda t, k=k, k8=k8, tt=tt, dh=dh, W0=W0:
                              t.matmul(
                                  psD_[tt][:, :],
                                  gqT_sb[:, k * FBL + tt * 128:
                                         k * FBL + (tt + 1) * 128],
                                  wA_sb[:, W0 + k8 * 512:W0 + (k8 + 1) * 512],
                                  start=(k == 0), stop=(k == 31)),
                              reads=[f"gqT{k}_{tt}", f"wA{sl_}"],
                              writes=[f"psD{tt}"])
                for tt in range(NTT):
                    E("v", lambda v, tt=tt, dh=dh: v.scalar_tensor_tensor(
                        ot_sb[:, tt * 512:(tt + 1) * 512],
                        psD_[tt][:, :], fst_sb[:, 12 + tt:13 + tt],
                        x2_sb[:, tt * D + dh * 512: tt * D + (dh + 1) * 512],
                        op0=ALU.mult, op1=ALU.add),
                      reads=[f"psD{tt}", f"sdv{tt}", f"x2{tt}"],
                      writes=[f"ot{tt}_{dh}"])
                    E("q", lambda q, tt=tt, dh=dh, fb0=fb0: q.dma_start(
                        out=out_d[fb0 + tt * 128: fb0 + (tt + 1) * 128,
                                  dh * 512:(dh + 1) * 512],
                        in_=ot_sb[:, tt * 512:(tt + 1) * 512]),
                      reads=[f"ot{tt}_{dh}"], writes=["outdr"], dma=True,
                      chan="qso")


        L.barrier()

        def play(engine, name):
            if name == "q":
                pid = engine.partition_id()
                PID[0] = [(pid % 4) == kk for kk in range(4)]
            for item in L.queue[name]:
                if item[0] == "wait":
                    engine.wait_ge(sems[item[1]], item[2])
                else:
                    _, fn, step, sem_name = item
                    fn(engine).then_inc(sems[sem_name], step)

        @block.vector
        def _(v):
            play(v, "v")

        @block.scalar
        def _(s):
            play(s, "s")

        @block.tensor
        def _(t):
            play(t, "t")

        @block.gpsimd
        def _(g):
            play(g, "g")

        @block.sync
        def _(q):
            play(q, "q")

    return nc


_CACHE = {}


def kernel(L0_attn_u, L0_attn_A_log, L0_attn_W_xdt, L0_attn_b_xdt,
           L0_attn_W_dt, L0_attn_b_dt, L0_attn_W_B, L0_attn_b_B,
           L0_attn_W_C, L0_attn_b_C, L0_attn_D, L0_ffn_w_gate,
           L0_ffn_w_up, L0_ffn_w_down):
    u = np.ascontiguousarray(np.asarray(L0_attn_u, dtype=np.float32))
    eye = np.eye(128, dtype=np.float32)
    in_maps = []
    for c in range(8):
        b, q = c // 4, c % 4
        perm = np.r_[np.arange(q * DSL, (q + 1) * DSL),
                     np.arange(0, q * DSL), np.arange((q + 1) * DSL, D)]
        im = {
            "u_full": np.ascontiguousarray(u[b][:, perm]),
            "u_tok": np.ascontiguousarray(u[b][q * TQ:(q + 1) * TQ, :]),
            "wxdtT": np.ascontiguousarray(
                np.asarray(L0_attn_W_xdt, np.float32).T[perm, :]),
            "bxdt": np.asarray(L0_attn_b_xdt, np.float32).reshape(RK, 1),
            "wdtT_sl": np.ascontiguousarray(
                np.asarray(L0_attn_W_dt, np.float32)[q * DSL:(q + 1) * DSL, :].T),
            "bdt_sl": np.ascontiguousarray(
                np.asarray(L0_attn_b_dt, np.float32)[q * DSL:(q + 1) * DSL]
                .reshape(DSL, 1)),
            "wbT": np.ascontiguousarray(
                np.asarray(L0_attn_W_B, np.float32).T[perm, :]),
            "bb": np.asarray(L0_attn_b_B, np.float32).reshape(NST, 1),
            "wcT": np.ascontiguousarray(
                np.asarray(L0_attn_W_C, np.float32).T[perm, :]),
            "bc": np.asarray(L0_attn_b_C, np.float32).reshape(NST, 1),
            "alog_sl": np.ascontiguousarray(
                np.asarray(L0_attn_A_log, np.float32)[q * DSL:(q + 1) * DSL, :]),
            "dvec": np.asarray(L0_attn_D, np.float32).reshape(1, D),
            "eye": eye,
            "wgT_q": np.ascontiguousarray(
                np.asarray(L0_ffn_w_gate, np.float32).T[:, q * 1024:(q + 1) * 1024]),
            "wuT_q": np.ascontiguousarray(
                np.asarray(L0_ffn_w_up, np.float32).T[:, q * 1024:(q + 1) * 1024]),
            "wdT_q": np.ascontiguousarray(
                np.asarray(L0_ffn_w_down, np.float32).T[q * 1024:(q + 1) * 1024, :]),
        }
        in_maps.append(im)
    if "nc" not in _CACHE:
        _CACHE["nc"] = build_program()
    nc = _CACHE["nc"]
    res = run_bass_kernel_spmd(nc, in_maps, core_ids=list(range(8)))
    _CACHE["res"] = res
    out = np.empty((B, T, D), dtype=np.float32)
    for c in range(8):
        b, q = c // 4, c % 4
        out[b, q * TQ:(q + 1) * TQ, :] = res.results[c]["out"]
    return out
